# version 03 (final)
# baseline (speedup 1.0000x reference)
import os
import numpy as np
from contextlib import ExitStack
import concourse.bass as bass
import concourse.mybir as mybir
from concourse.bass_utils import run_bass_kernel_spmd

F32 = mybir.dt.float32
BF16 = mybir.dt.bfloat16
I32 = mybir.dt.int32
AF = mybir.ActivationFunctionType
ALU = mybir.AluOpType
AX = mybir.AxisListType

ENGS = ("sync", "vector", "scalar", "tensor", "gpsimd")


class Ctr:
    def __init__(self, handle):
        self.h = handle
        self.v = 0

    def bump(self, n=1):
        self.v += n
        return self.v


class Prog:

    def __init__(self, nc, stack):
        self.nc = nc
        self.stack = stack
        self.ops = {e: [] for e in ENGS}
        self._semn = 0
        self._junk = {}
        for eng in ("vector", "scalar", "gpsimd"):
            self._junk[eng] = stack.enter_context(nc.sbuf_tensor(f"junk_{eng}", [1, 2], F32))

    def sem(self, name=None):
        name = name or f"s{self._semn}"
        self._semn += 1
        return Ctr(self.stack.enter_context(self.nc.semaphore(name)))

    def do(self, eng, fn, inc=None, inc_n=1):
        tgt = inc.bump(inc_n) if inc is not None else None
        def op(e, fn=fn, inc=inc, inc_n=inc_n):
            r = fn(e)
            if inc is not None:
                r.then_inc(inc.h, inc_n)
        self.ops[eng].append(op)
        return tgt

    def wait(self, eng, ctr, val=None):
        v = ctr.v if val is None else val
        if v <= 0:
            return
        self.ops[eng].append(lambda e, c=ctr, v=v: e.wait_ge(c.h, v))

    def dof(self, eng, fn, inc=None, inc_n=1):
        self.do(eng, fn)
        j = self._junk[eng]
        if eng == "scalar":
            return self.do(eng, lambda e: e.copy(out=j[0:1, 0:1], in_=j[0:1, 1:2]),
                           inc=inc, inc_n=inc_n)
        return self.do(eng, lambda e: e.tensor_copy(out=j[0:1, 0:1], in_=j[0:1, 1:2]),
                       inc=inc, inc_n=inc_n)

    def dma(self, eng, out, in_, ctr, **kw):
        q = getattr(ctr, "queue", None)
        if q is None:
            ctr.queue = eng
        else:
            assert q == eng, f"DMA counter used on two queues: {q} and {eng}"
        return self.do(eng, lambda e, o=out, i=in_, kw=kw: e.dma_start(out=o, in_=i, **kw),
                       inc=ctr, inc_n=16)

    def emit(self, block):
        for eng in ENGS:
            lst = self.ops[eng]
            junk = self._junk.get(eng)
            def body(e, lst=lst, junk=junk, eng=eng):
                if junk is not None:
                    if eng == "scalar":
                        e.memzero(junk[:])
                    else:
                        e.memset(junk[:], 0.0)
                for op in lst:
                    op(e)
            getattr(block, eng)(body)

    def n_ops(self):
        return {e: len(v) for e, v in self.ops.items()}


class Banks:

    def __init__(self, nc, stack, n, width, name="pb", dt=F32):
        self.b = []
        for i in range(n):
            t = stack.enter_context(nc.psum_tensor(f"{name}{i}", [128, width], dt))
            self.b.append([t, None, 0])
        self.i = 0

    def acquire(self, P, eng_producer):
        t, ctr, val = self.b[self.i % len(self.b)]
        idx = self.i % len(self.b)
        self.i += 1
        if ctr is not None and val > 0:
            P.wait(eng_producer, ctr, val)
        return idx, t

    def release(self, idx, ctr, val):
        self.b[idx][1] = ctr
        self.b[idx][2] = val


def sb(nc, stack, name, shape, dt=F32):
    return stack.enter_context(nc.sbuf_tensor(name, list(shape), dt))


def ceil_div(a, b):
    return (a + b - 1) // b


def build_p0(T=4096, TC=512):
    NCH = T // TC
    KC = 8
    nc = bass.Bass("TRN2", target_bir_lowering=False)
    x0 = nc.dram_tensor("x0", [1024, T], F32, kind="ExternalInput").ap()
    rkvT = nc.dram_tensor("rkvT", [1024, 3072], F32, kind="ExternalInput").ap()
    outT = nc.dram_tensor("outT", [1024, 1024], F32, kind="ExternalInput").ap()
    tdec = nc.dram_tensor("tdec", [128, 8], F32, kind="ExternalInput").ap()
    tfirst = nc.dram_tensor("tfirst", [128, 8], F32, kind="ExternalInput").ap()
    spart = nc.dram_tensor("spart", [1024, T], F32, kind="ExternalOutput").ap()
    rkv16 = nc.dram_tensor("rkv16", [1024, 3072], BF16, kind="Internal").ap()

    with ExitStack() as st:
        P = Prog(nc, st)
        emit_p0(nc, st, P, x0, rkvT, outT, tdec, tfirst, spart, rkv16, T, TC)
        with nc.Block() as block:
            P.emit(block)
    return nc


def emit_p0(nc, st, P, x0, rkvT, outT, tdec, tfirst, spart, rkv16, T, TC):
    NCH = T // TC
    KC = 8
    xc = [sb(nc, st, f"p0xc{i}", (128, KC * TC)) for i in range(2)]
    xb = [sb(nc, st, f"p0xb{i}", (128, KC * TC), BF16) for i in range(2)]
    xnb = [sb(nc, st, f"p0xnb{i}", (128, KC * TC), BF16) for i in range(2)]
    a_t = sb(nc, st, "p0a_t", (128, KC * TC))
    acol = sb(nc, st, "p0acol", (128, 8))
    eu = sb(nc, st, "p0eu", (128, 8))
    onev = sb(nc, st, "p0onev", (128, TC))
    oneb = sb(nc, st, "p0oneb", (128, 1), BF16)
    onerow = sb(nc, st, "p0onerow", (1, 128), BF16)
    mrow = sb(nc, st, "p0mrow", (1, 3 * TC))
    row16 = sb(nc, st, "p0row16", (1, 2 * TC), BF16)
    rstdb = sb(nc, st, "p0rstdb", (128, TC))
    mrb = sb(nc, st, "p0mrb", (128, TC))
    ek = [sb(nc, st, f"p0ek{i}", (128, TC)) for i in range(2)]
    sr = [sb(nc, st, f"p0sr{i}", (128, TC)) for i in range(2)]
    zv = sb(nc, st, "p0zv", (128, TC))
    num = sb(nc, st, "p0num", (128, TC + 2))
    den = sb(nc, st, "p0den", (128, TC + 2))
    seeds = sb(nc, st, "p0seeds", (128, 16))
    numer = [sb(nc, st, f"p0numer{i}", (128, TC)) for i in range(2)]
    denom = sb(nc, st, "p0denom", (128, TC))
    outh = [sb(nc, st, f"p0outh{i}", (128, KC * TC), BF16) for i in range(2)]
    spt = [sb(nc, st, f"p0spt{i}", (128, KC * TC)) for i in range(2)]
    ot16 = sb(nc, st, "p0ot16", (128, KC * 1024), BF16)
    wbuf = [sb(nc, st, f"p0wbuf{i}", (128, KC * 128), BF16) for i in range(4)]
    wcvt = sb(nc, st, "p0wcvt", (128, 3072), BF16)
    tdv = sb(nc, st, "p0tdv", (128, 8))
    tfv = sb(nc, st, "p0tfv", (128, 8))

    banks = Banks(nc, st, 6, TC, "p0pb")
    pst = [st.enter_context(nc.psum_tensor(f"p0ps{i}", [128, TC], F32)) for i in range(2)]

    dl = P.sem(); dw = P.sem(); dcg = P.sem(); dcs = P.sem(); do = P.sem()
    pe = P.sem(); dv = P.sem(); ac = P.sem(); po = P.sem()
    P.dma("sync", tdv[:], tdec, dl)
    P.dma("sync", tfv[:], tfirst, dl)
    t_vecs = dl.v
    P.dma("gpsimd", ot16[:].rearrange("p (kc j) -> p kc j", j=1024), outT.rearrange("(kc p) j -> p kc j", p=128), dcg)
    t_ot = dcg.v
    for k in range(KC):
        P.dma("gpsimd", wcvt[:], rkvT[k * 128:(k + 1) * 128, :], dcg)
        P.wait("sync", dcg)
        P.dma("sync", rkv16[k * 128:(k + 1) * 128, :], wcvt[:], dcs)
        if k < KC - 1:
            P.wait("gpsimd", dcs)
    t_prep = dcs.v

    P.wait("scalar", dl, t_vecs)
    P.do("scalar", lambda e: e.activation(out=acol[:], in_=tdv[:], func=AF.Exp), inc=ac)
    P.do("scalar", lambda e: e.activation(out=acol[:], in_=acol[:], func=AF.Exp, scale=-1.0), inc=ac)
    P.dof("scalar", lambda e: e.activation(out=eu[:], in_=tfv[:], func=AF.Exp), inc=ac)
    t_acol = ac.v
    P.do("vector", lambda e: e.memset(onev[:], 1.0), inc=dv)
    P.do("vector", lambda e: e.memset(oneb[:], 1.0), inc=dv)
    P.do("vector", lambda e: e.memset(onerow[:], 1.0), inc=dv)
    P.do("vector", lambda e: e.memset(seeds[:], 0.0), inc=dv)
    P.wait("vector", ac, t_acol)
    for k in range(KC):
        P.do("vector", lambda e, k=k: e.tensor_scalar(
            out=a_t[:, k * TC:(k + 1) * TC], in0=onev[:], scalar1=acol[:, k:k + 1],
            scalar2=None, op0=ALU.mult), inc=dv)
    t_const = dv.v

    xc_free = [None, None]
    xb_free = [None, None]
    xnb_free = [None, None]
    outh_free = [None, None]
    spt_free = [None, None]
    wbuf_free = [None] * 4
    ek_free = [None, None]
    sr_free = [None, None]
    numer_free = [None, None]
    pst_free = None
    t_outh_ready = {}

    P.wait("sync", dcs, t_prep)
    for c in range(NCH):
        t0 = c * TC
        cb = c % 2
        if xc_free[cb] is not None:
            P.wait("sync", *xc_free[cb])
        P.dma("sync", xc[cb][:].rearrange("p (k t) -> p k t", t=TC), x0[:, t0:t0 + TC].rearrange("(k p) t -> p k t", p=128), dl)
        x_ld = dl.v
        P.wait("vector", dl, x_ld)
        if xb_free[cb] is not None:
            P.wait("vector", *xb_free[cb])
        P.do("vector", lambda e, cb=cb: e.tensor_copy(out=xb[cb][:], in_=xc[cb][:]), inc=dv)
        t_cast = dv.v
        P.wait("scalar", dl, x_ld)
        if xnb_free[cb] is not None:
            P.wait("scalar", *xnb_free[cb])
        P.do("scalar", lambda e, cb=cb: e.activation(out=xnb[cb][:], in_=xc[cb][:], func=AF.Square), inc=ac)
        t_sq = ac.v
        P.wait("tensor", dv, t_cast)
        P.wait("tensor", ac, t_sq)
        if pst_free is not None:
            P.wait("tensor", *pst_free)
        for k in range(KC):
            P.do("tensor", lambda e, cb=cb, k=k: e.matmul(
                pst[0][0:1, :], oneb[:, 0:1], xb[cb][:, k * TC:(k + 1) * TC],
                start=(k == 0), stop=(k == KC - 1)),
                inc=pe if k == KC - 1 else None)
        for k in range(KC):
            P.do("tensor", lambda e, cb=cb, k=k: e.matmul(
                pst[1][0:1, :], oneb[:, 0:1], xnb[cb][:, k * TC:(k + 1) * TC],
                start=(k == 0), stop=(k == KC - 1)),
                inc=pe if k == KC - 1 else None)
        t_stats = pe.v
        mean = mrow[0:1, 0:TC]; mr = mrow[0:1, TC:2 * TC]; rstd = mrow[0:1, 2 * TC:3 * TC]
        P.wait("vector", pe, t_stats)
        P.do("vector", lambda e: e.tensor_scalar(out=mean, in0=pst[0][0:1, :],
                                                 scalar1=1.0 / 1024, scalar2=None, op0=ALU.mult), inc=dv)
        P.do("vector", lambda e: e.tensor_tensor(out=mr, in0=mean, in1=mean, op=ALU.mult), inc=dv)
        P.do("vector", lambda e: e.tensor_scalar(out=mr, in0=mr, scalar1=-1e-5, scalar2=None,
                                                 op0=ALU.add), inc=dv)
        P.dof("vector", lambda e: e.scalar_tensor_tensor(out=rstd, in0=pst[1][0:1, :],
                                                         scalar=1.0 / 1024, in1=mr,
                                                         op0=ALU.mult, op1=ALU.subtract), inc=dv)
        t_var = dv.v
        P.wait("scalar", dv, t_var)
        P.dof("scalar", lambda e: e.activation(out=rstd, in_=rstd, func=AF.Sqrt), inc=ac)
        t_sqrt = ac.v
        P.wait("vector", ac, t_sqrt)
        P.do("vector", lambda e: e.reciprocal(out=rstd, in_=rstd), inc=dv)
        P.do("vector", lambda e: e.tensor_tensor(out=mr, in0=mean, in1=rstd, op=ALU.mult), inc=dv)
        P.do("vector", lambda e: e.tensor_copy(out=row16[0:1, 0:TC], in_=rstd), inc=dv)
        P.dof("vector", lambda e: e.tensor_copy(out=row16[0:1, TC:2 * TC], in_=mr), inc=dv)
        t_rows = dv.v
        P.wait("tensor", dv, t_rows)
        P.do("tensor", lambda e: e.matmul(pst[0][:, :], onerow[0:1, :], row16[0:1, 0:TC],
                                          start=True, stop=True), inc=pe)
        P.do("tensor", lambda e: e.matmul(pst[1][:, :], onerow[0:1, :], row16[0:1, TC:2 * TC],
                                          start=True, stop=True), inc=pe)
        t_bc = pe.v
        P.wait("vector", pe, t_bc)
        P.do("vector", lambda e: e.tensor_copy(out=rstdb[:], in_=pst[0][:, :]), inc=dv)
        P.do("vector", lambda e: e.tensor_copy(out=mrb[:], in_=pst[1][:, :]), inc=dv)
        pst_free = (dv, dv.v)
        xb_free[cb] = (pe, t_stats)
        if spt_free[cb] is not None:
            P.wait("vector", *spt_free[cb])
        for k in range(KC):
            P.do("vector", lambda e, cb=cb, k=k: e.tensor_tensor(
                out=spt[cb][:, k * TC:(k + 1) * TC], in0=xc[cb][:, k * TC:(k + 1) * TC],
                in1=rstdb[:], op=ALU.mult), inc=dv)
            if k < KC - 1:
                P.do("vector", lambda e, cb=cb, k=k: e.tensor_tensor(
                    out=xnb[cb][:, k * TC:(k + 1) * TC],
                    in0=spt[cb][:, k * TC:(k + 1) * TC], in1=mrb[:], op=ALU.subtract), inc=dv)
            else:
                P.dof("vector", lambda e, cb=cb, k=k: e.tensor_tensor(
                    out=xnb[cb][:, k * TC:(k + 1) * TC],
                    in0=spt[cb][:, k * TC:(k + 1) * TC], in1=mrb[:], op=ALU.subtract), inc=dv)
        t_xn = dv.v
        xc_free[cb] = (dv, t_xn)

        for k in range(KC):
            outs = {}
            for j, mrow_idx in enumerate((8 + k, 16 + k, k)):
                wb = (c * KC * 3 + k * 3 + j) % 4
                if wbuf_free[wb] is not None:
                    P.wait("sync", *wbuf_free[wb])
                P.dma("sync",
                      wbuf[wb][:].rearrange("p (kc j) -> p kc j", j=128),
                      rkv16[:, mrow_idx * 128:(mrow_idx + 1) * 128]
                      .rearrange("(kc p) j -> p kc j", p=128),
                      dw)
                w_ld = dw.v
                bi, bt = banks.acquire(P, "tensor")
                P.wait("tensor", dw, w_ld)
                if j == 0 and k == 0:
                    P.wait("tensor", dv, t_xn)
                for kc in range(KC):
                    P.do("tensor", lambda e, cb=cb, kc=kc, wb=wb, bt=bt: e.matmul(
                        bt[:, :], wbuf[wb][:, kc * 128:(kc + 1) * 128],
                        xnb[cb][:, kc * TC:(kc + 1) * TC],
                        start=(kc == 0), stop=(kc == KC - 1)),
                        inc=pe if kc == KC - 1 else None)
                t_mm = pe.v
                wbuf_free[wb] = (pe, t_mm)
                outs[("k", "v", "r")[j]] = (bi, bt, t_mm)
            kb = k % 2
            bi, bt, t_mm = outs["k"]
            P.wait("scalar", pe, t_mm)
            if ek_free[kb] is not None:
                P.wait("scalar", *ek_free[kb])
            t_ek = P.dof("scalar", lambda e, bt=bt, kb=kb: e.activation(
                out=ek[kb][:], in_=bt[:, :], func=AF.Exp), inc=ac)
            banks.release(bi, ac, t_ek)
            bi, bt, t_mm = outs["v"]
            P.wait("vector", pe, t_mm)
            P.wait("vector", ac, t_ek)
            t_zv = P.do("vector", lambda e, bt=bt, kb=kb: e.tensor_tensor(
                out=zv[:], in0=ek[kb][:], in1=bt[:, :], op=ALU.mult), inc=dv)
            banks.release(bi, dv, t_zv)
            bi, bt, t_mm = outs["r"]
            P.wait("scalar", pe, t_mm)
            if sr_free[kb] is not None:
                P.wait("scalar", *sr_free[kb])
            t_sr = P.dof("scalar", lambda e, bt=bt, kb=kb: e.activation(
                out=sr[kb][:], in_=bt[:, :], func=AF.Sigmoid), inc=ac)
            banks.release(bi, ac, t_sr)
            P.do("vector", lambda e, k=k: e.tensor_tensor_scan(
                out=num[:, 1:1 + TC], data0=a_t[:, k * TC:(k + 1) * TC], data1=zv[:],
                initial=seeds[:, k:k + 1], op0=ALU.mult, op1=ALU.add), inc=dv)
            P.do("vector", lambda e, k=k: e.tensor_copy(out=num[:, 0:1], in_=seeds[:, k:k + 1]), inc=dv)
            P.do("vector", lambda e, k=k, kb=kb: e.tensor_tensor_scan(
                out=den[:, 1:1 + TC], data0=a_t[:, k * TC:(k + 1) * TC], data1=ek[kb][:],
                initial=seeds[:, 8 + k:9 + k], op0=ALU.mult, op1=ALU.add), inc=dv)
            P.do("vector", lambda e, k=k: e.tensor_copy(out=den[:, 0:1], in_=seeds[:, 8 + k:9 + k]), inc=dv)
            if numer_free[kb] is not None:
                P.wait("vector", *numer_free[kb])
            P.do("vector", lambda e, k=k, kb=kb: e.scalar_tensor_tensor(
                out=numer[kb][:], in0=zv[:], scalar=eu[:, k:k + 1], in1=num[:, 0:TC],
                op0=ALU.mult, op1=ALU.add), inc=dv)
            P.do("vector", lambda e, k=k, kb=kb: e.scalar_tensor_tensor(
                out=denom[:], in0=ek[kb][:], scalar=eu[:, k:k + 1], in1=den[:, 0:TC],
                op0=ALU.mult, op1=ALU.add), inc=dv)
            ek_free[kb] = (dv, dv.v)
            P.do("vector", lambda e, k=k: e.tensor_copy(out=seeds[:, k:k + 1], in_=num[:, TC:TC + 1]), inc=dv)
            P.do("vector", lambda e, k=k: e.tensor_copy(out=seeds[:, 8 + k:9 + k], in_=den[:, TC:TC + 1]), inc=dv)
            P.do("vector", lambda e: e.reciprocal(out=denom[:], in_=denom[:]), inc=dv)
            P.dof("vector", lambda e, kb=kb: e.tensor_tensor(out=numer[kb][:], in0=numer[kb][:],
                                                             in1=denom[:], op=ALU.mult), inc=dv)
            t_y = dv.v
            if k == 0 and outh_free[cb] is not None:
                P.wait("gpsimd", *outh_free[cb])
            P.wait("gpsimd", dv, t_y)
            P.wait("gpsimd", ac, t_sr)
            t_oh = P.dof("gpsimd", lambda e, cb=cb, k=k, kb=kb: e.tensor_tensor(
                out=outh[cb][:, k * TC:(k + 1) * TC], in0=sr[kb][:], in1=numer[kb][:],
                op=ALU.mult), inc=po)
            t_outh_ready[(c, k)] = t_oh
            sr_free[kb] = (po, t_oh)
            numer_free[kb] = (po, t_oh)

        xnb_free[cb] = (pe, pe.v)
        P.wait("tensor", po, t_outh_ready[(c, KC - 1)])
        if c == 0:
            P.wait("tensor", dcg, t_ot)
        for m in range(KC):
            bi, bt = banks.acquire(P, "tensor")
            for kc in range(KC):
                P.do("tensor", lambda e, cb=cb, m=m, kc=kc, bt=bt: e.matmul(
                    bt[:, :], ot16[:, kc * 1024 + m * 128: kc * 1024 + (m + 1) * 128],
                    outh[cb][:, kc * TC:(kc + 1) * TC],
                    start=(kc == 0), stop=(kc == KC - 1)),
                    inc=pe if kc == KC - 1 else None)
            t_mm = pe.v
            P.wait("vector", pe, t_mm)
            if m < KC - 1:
                t_ev = P.do("vector", lambda e, cb=cb, m=m, bt=bt: e.tensor_copy(
                    out=spt[cb][:, m * TC:(m + 1) * TC], in_=bt[:, :]), inc=dv)
            else:
                t_ev = P.dof("vector", lambda e, cb=cb, m=m, bt=bt: e.tensor_copy(
                    out=spt[cb][:, m * TC:(m + 1) * TC], in_=bt[:, :]), inc=dv)
            banks.release(bi, dv, t_ev)
        t_sp = dv.v
        outh_free[cb] = (pe, pe.v)
        P.wait("sync", dv, t_sp)
        P.dma("sync", spart[:, t0:t0 + TC].rearrange("(k p) t -> p k t", p=128),
              spt[cb][:].rearrange("p (k t) -> p k t", t=TC), do)
        spt_free[cb] = (do, do.v)

    P.wait("sync", do)
    return P


def stage_p0(inputs):
    x = np.asarray(inputs["L0_attn_x"])
    rkv_w = np.asarray(inputs["L0_attn_rkv_w"])
    out_w = np.asarray(inputs["L0_attn_out_w"])
    tdec = {0: np.asarray(inputs["L0_attn_time_decay"]),
            1: np.asarray(inputs["L0_attn_time_decay_rev"])}
    tfir = {0: np.asarray(inputs["L0_attn_time_first"]),
            1: np.asarray(inputs["L0_attn_time_first_rev"])}
    rkvT = np.ascontiguousarray(rkv_w.T)
    outT = np.ascontiguousarray(out_w.T)
    maps = []
    for core in range(8):
        b, p = core // 2, core & 1
        xT = np.ascontiguousarray(x[b].T)
        if p == 1:
            xT = np.ascontiguousarray(xT[:, ::-1])
        maps.append({
            "x0": xT,
            "rkvT": np.ascontiguousarray(rkvT[:, p * 3072:(p + 1) * 3072]),
            "outT": np.ascontiguousarray(outT[p * 1024:(p + 1) * 1024, :]),
            "tdec": np.ascontiguousarray(tdec[p].reshape(8, 128).T),
            "tfirst": np.ascontiguousarray(tfir[p].reshape(8, 128).T),
        })
    return maps


def unstage_p0(results):
    outs = []
    for core in range(8):
        sp = np.asarray(results[core]["spart"])
        if core & 1:
            sp = sp[:, ::-1]
        outs.append(sp)
    return outs


def emit_ln(P, xc_ap, out_b, scratch, pst, sems, TC, KC=8, pre=()):
    pe, dv, ac = sems
    for eng, ctr, val in pre:
        P.wait(eng, ctr, val)
    oneb = scratch["oneb"]; onerow = scratch["onerow"]
    mrow = scratch["mrow"]; row16 = scratch["row16"]
    rstdb = scratch["rstdb"]; mrb = scratch["mrb"]; xsq = scratch["xsq"]
    xb16 = out_b
    tmp = scratch["lntmp"]
    P.do("vector", lambda e: e.tensor_copy(out=xb16[:], in_=xc_ap), inc=dv)
    t_cast = dv.v
    P.do("scalar", lambda e: e.activation(out=xsq[:], in_=xc_ap, func=AF.Square), inc=ac)
    t_sq = ac.v
    P.wait("tensor", dv, t_cast)
    P.wait("tensor", ac, t_sq)
    for k in range(KC):
        P.do("tensor", lambda e, k=k: e.matmul(
            pst[0][0:1, :], oneb[:, 0:1], xb16[:, k * TC:(k + 1) * TC],
            start=(k == 0), stop=(k == KC - 1)), inc=pe if k == KC - 1 else None)
    for k in range(KC):
        P.do("tensor", lambda e, k=k: e.matmul(
            pst[1][0:1, :], oneb[:, 0:1], xsq[:, k * TC:(k + 1) * TC],
            start=(k == 0), stop=(k == KC - 1)), inc=pe if k == KC - 1 else None)
    t_stats = pe.v
    mean = mrow[0:1, 0:TC]; mr = mrow[0:1, TC:2 * TC]; rstd = mrow[0:1, 2 * TC:3 * TC]
    P.wait("vector", pe, t_stats)
    P.do("vector", lambda e: e.tensor_scalar(out=mean, in0=pst[0][0:1, :],
                                             scalar1=1.0 / 1024, scalar2=None, op0=ALU.mult), inc=dv)
    P.do("vector", lambda e: e.tensor_tensor(out=mr, in0=mean, in1=mean, op=ALU.mult), inc=dv)
    P.do("vector", lambda e: e.tensor_scalar(out=mr, in0=mr, scalar1=-1e-5, scalar2=None,
                                             op0=ALU.add), inc=dv)
    P.dof("vector", lambda e: e.scalar_tensor_tensor(out=rstd, in0=pst[1][0:1, :],
                                                     scalar=1.0 / 1024, in1=mr,
                                                     op0=ALU.mult, op1=ALU.subtract), inc=dv)
    t_var = dv.v
    P.wait("scalar", dv, t_var)
    P.dof("scalar", lambda e: e.activation(out=rstd, in_=rstd, func=AF.Sqrt), inc=ac)
    t_sqrt = ac.v
    P.wait("vector", ac, t_sqrt)
    P.do("vector", lambda e: e.reciprocal(out=rstd, in_=rstd), inc=dv)
    P.do("vector", lambda e: e.tensor_tensor(out=mr, in0=mean, in1=rstd, op=ALU.mult), inc=dv)
    P.do("vector", lambda e: e.tensor_copy(out=row16[0:1, 0:TC], in_=rstd), inc=dv)
    P.dof("vector", lambda e: e.tensor_copy(out=row16[0:1, TC:2 * TC], in_=mr), inc=dv)
    t_rows = dv.v
    P.wait("tensor", dv, t_rows)
    P.do("tensor", lambda e: e.matmul(pst[0][:, :], onerow[0:1, :], row16[0:1, 0:TC],
                                      start=True, stop=True), inc=pe)
    P.do("tensor", lambda e: e.matmul(pst[1][:, :], onerow[0:1, :], row16[0:1, TC:2 * TC],
                                      start=True, stop=True), inc=pe)
    t_bc = pe.v
    P.wait("vector", pe, t_bc)
    P.do("vector", lambda e: e.tensor_copy(out=rstdb[:], in_=pst[0][:, :]), inc=dv)
    P.do("vector", lambda e: e.tensor_copy(out=mrb[:], in_=pst[1][:, :]), inc=dv)
    for k in range(KC):
        P.do("vector", lambda e, k=k: e.tensor_tensor(
            out=tmp[:, 0:TC], in0=xc_ap[:, k * TC:(k + 1) * TC],
            in1=rstdb[:], op=ALU.mult), inc=dv)
        if k < KC - 1:
            P.do("vector", lambda e, k=k: e.tensor_tensor(
                out=out_b[:, k * TC:(k + 1) * TC], in0=tmp[:, 0:TC], in1=mrb[:],
                op=ALU.subtract), inc=dv)
        else:
            P.dof("vector", lambda e, k=k: e.tensor_tensor(
                out=out_b[:, k * TC:(k + 1) * TC], in0=tmp[:, 0:TC], in1=mrb[:],
                op=ALU.subtract), inc=dv)
    return dv.v, t_stats


def ln_scratch(nc, st, prefix, TC, KC=8):
    return {
        "oneb": sb(nc, st, prefix + "oneb", (128, 1), BF16),
        "onerow": sb(nc, st, prefix + "onerow", (1, 128), BF16),
        "mrow": sb(nc, st, prefix + "mrow", (1, 3 * TC)),
        "row16": sb(nc, st, prefix + "row16", (1, 2 * TC), BF16),
        "rstdb": sb(nc, st, prefix + "rstdb", (128, TC)),
        "mrb": sb(nc, st, prefix + "mrb", (128, TC)),
        "xsq": sb(nc, st, prefix + "xsq", (128, KC * TC), BF16),
        "lntmp": sb(nc, st, prefix + "lntmp", (128, TC)),
    }


def ln_scratch_init(P, scratch, dv):
    P.do("vector", lambda e: e.memset(scratch["oneb"][:], 1.0), inc=dv)
    P.do("vector", lambda e: e.memset(scratch["onerow"][:], 1.0), inc=dv)


def emit_weff(nc, st, P, wT_dram, weff_dram, din, dout, prefix, sems, dmas, pbanks):
    KCW = din // 128
    NT = dout // 512
    pe, dv, ac = sems
    dwl, dco = dmas
    wraw = [sb(nc, st, f"{prefix}wraw{i}", (128, dout)) for i in range(2)]
    wabs = [sb(nc, st, f"{prefix}wabs{i}", (128, dout)) for i in range(2)]
    onef = sb(nc, st, f"{prefix}onef", (128, 1))
    onerow = sb(nc, st, f"{prefix}onerow", (1, 128))
    srow = sb(nc, st, f"{prefix}srow", (1, 2 * dout))
    scb = sb(nc, st, f"{prefix}scb", (128, dout))
    invb = sb(nc, st, f"{prefix}invb", (128, dout))
    weff = sb(nc, st, f"{prefix}weff", (128, dout), BF16)
    P.do("vector", lambda e: e.memset(onef[:], 1.0), inc=dv)
    P.do("vector", lambda e: e.memset(onerow[:], 1.0), inc=dv)
    t_ones = dv.v
    abs_done = {}
    mm_done = {}
    for kc in range(KCW):
        wb = kc % 2
        if kc >= 2:
            P.wait("sync", ac, abs_done[kc - 2])
        P.dma("sync", wraw[wb][:], wT_dram[kc * 128:(kc + 1) * 128, :], dwl)
        t_w = dwl.v
        P.wait("scalar", dwl, t_w)
        if kc >= 2:
            P.wait("scalar", pe, mm_done[kc - 2])
        abs_done[kc] = P.do("scalar", lambda e, wb=wb: e.activation(
            out=wabs[wb][:], in_=wraw[wb][:], func=AF.Abs), inc=ac)
        P.wait("tensor", ac, abs_done[kc])
        if kc == 0:
            P.wait("tensor", dv, t_ones)
        for nt in range(NT):
            bk = pbanks[nt]
            P.do("tensor", lambda e, wb=wb, nt=nt, bk=bk, kc=kc: e.matmul(
                bk[0:1, :], onef[:, 0:1], wabs[wb][:, nt * 512:(nt + 1) * 512],
                start=(kc == 0), stop=(kc == KCW - 1)),
                inc=pe if nt == NT - 1 else None)
        mm_done[kc] = pe.v
    t_scale_mm = pe.v
    P.wait("vector", pe, t_scale_mm)
    for nt in range(NT):
        bk = pbanks[nt]
        P.do("vector", lambda e, nt=nt, bk=bk: e.tensor_scalar(
            out=srow[0:1, nt * 512:(nt + 1) * 512], in0=bk[0:1, :],
            scalar1=1.0 / din, scalar2=1e-5, op0=ALU.mult, op1=ALU.max), inc=dv)
    P.do("vector", lambda e: e.reciprocal(out=srow[0:1, dout:2 * dout], in_=srow[0:1, 0:dout]), inc=dv)
    t_srow = dv.v
    P.wait("tensor", dv, t_srow)
    for j, dst in enumerate((scb, invb)):
        for nt in range(NT):
            bk = pbanks[0]
            if j > 0 or nt > 0:
                P.wait("tensor", dv, dv.v)
            P.do("tensor", lambda e, j=j, nt=nt, bk=bk: e.matmul(
                bk[:, :], onerow[0:1, :],
                srow[0:1, j * dout + nt * 512: j * dout + (nt + 1) * 512],
                start=True, stop=True), inc=pe)
            t_b = pe.v
            P.wait("vector", pe, t_b)
            P.do("vector", lambda e, dst=dst, nt=nt, bk=bk: e.tensor_copy(
                out=dst[:, nt * 512:(nt + 1) * 512], in_=bk[:, :]), inc=dv)
    t_bcast = dv.v
    banks_free = (dv, t_bcast)
    sgn_done = {}
    tt_done = {}
    out_done = {}
    for kc in range(KCW):
        wb = kc % 2
        if kc >= 2:
            P.wait("sync", ac, sgn_done[kc - 2])
        P.dma("sync", wraw[wb][:], wT_dram[kc * 128:(kc + 1) * 128, :], dwl)
        t_w = dwl.v
        P.wait("scalar", dwl, t_w)
        if kc >= 2:
            P.wait("scalar", dv, tt_done[kc - 2])
        t_abs = P.do("scalar", lambda e, wb=wb: e.activation(
            out=wabs[wb][:], in_=wraw[wb][:], func=AF.Abs), inc=ac)
        t_sgn = P.do("scalar", lambda e, wb=wb: e.activation(
            out=wraw[wb][:], in_=wraw[wb][:], func=AF.Sign), inc=ac)
        sgn_done[kc] = t_sgn
        P.wait("vector", ac, t_sgn)
        if kc == 0:
            P.wait("vector", dv, t_bcast)
        P.do("vector", lambda e, wb=wb: e.tensor_tensor(
            out=wabs[wb][:], in0=wabs[wb][:], in1=invb[:], op=ALU.mult), inc=dv)
        P.do("vector", lambda e, wb=wb: e.tensor_scalar(
            out=wabs[wb][:], in0=wabs[wb][:], scalar1=0.5, scalar2=None, op0=ALU.is_ge), inc=dv)
        P.do("vector", lambda e, wb=wb: e.tensor_tensor(
            out=wabs[wb][:], in0=wabs[wb][:], in1=wraw[wb][:], op=ALU.mult), inc=dv)
        if kc >= 1:
            P.wait("vector", dco, out_done[kc - 1])
        P.do("vector", lambda e, wb=wb: e.tensor_tensor(
            out=weff[:], in0=wabs[wb][:], in1=scb[:], op=ALU.mult), inc=dv)
        tt_done[kc] = dv.v
        P.wait("sync", dv, tt_done[kc])
        P.dma("sync", weff_dram[kc * 128:(kc + 1) * 128, :], weff[:], dco)
        out_done[kc] = dco.v
    P.wait("sync", dco)
    return dco.v, banks_free


def build_p1(TH=2048, TC=512):
    nc = bass.Bass("TRN2", target_bir_lowering=False)
    KC = 8
    xh = nc.dram_tensor("xh", [1024, TH], F32, kind="ExternalInput").ap()
    sfh = nc.dram_tensor("sfh", [1024, TH], F32, kind="ExternalInput").ap()
    sbh = nc.dram_tensor("sbh", [1024, TH], F32, kind="ExternalInput").ap()
    w_fcT = nc.dram_tensor("w_fcT", [1024, 4096], F32, kind="ExternalInput").ap()
    w_projT = nc.dram_tensor("w_projT", [4096, 1024], F32, kind="ExternalInput").ap()
    vT = nc.dram_tensor("vT", [128, 128], F32, kind="ExternalInput").ap()
    fbT = nc.dram_tensor("fbT", [16, 16], F32, kind="ExternalInput").ap()
    uT = nc.dram_tensor("uT", [16, 4096], F32, kind="ExternalInput").ap()
    gate16 = nc.dram_tensor("gate16", [16, 1], F32, kind="ExternalInput").ap()
    W_in = nc.dram_tensor("W_in", [1024, 4096], F32, kind="ExternalInput").ap()
    x1o = nc.dram_tensor("x1o", [1024, TH], F32, kind="ExternalOutput").ap()
    xzo = nc.dram_tensor("xzo", [4096, TH], F32, kind="ExternalOutput").ap()
    dbg = {}
    _os = os
    if _os.environ.get("P1DEBUG"):
        for nm, rows in [("g0o", 1024), ("s0o", 1024), ("xn0o", 1024), ("rro", 4096),
                         ("t1o", 16), ("t2o", 16)]:
            dbg[nm] = nc.dram_tensor(nm, [rows, TH], F32, kind="ExternalOutput").ap()
    weff_fc = nc.dram_tensor("weff_fc", [1024, 4096], BF16, kind="Internal").ap()
    weff_pj = nc.dram_tensor("weff_pj", [4096, 1024], BF16, kind="Internal").ap()

    with ExitStack() as st:
        P = Prog(nc, st)
        emit_p1(nc, st, P, dict(xh=xh, sfh=sfh, sbh=sbh, w_fcT=w_fcT, w_projT=w_projT,
                                vT=vT, fbT=fbT, uT=uT, gate16=gate16, W_in=W_in,
                                x1o=x1o, xzo=xzo, weff_fc=weff_fc, weff_pj=weff_pj,
                                dbg=dbg),
                TH, TC)
        with nc.Block() as block:
            P.emit(block)
    return nc


def emit_p1(nc, st, P, io, TH, TC):
    KC = 8
    MF = 32
    MP = 8
    NCH = TH // TC
    GRP = [list(range(g, min(g + 4, MF))) for g in range(0, MF, 4)]

    banks = Banks(nc, st, 6, TC, "p1pb")
    pst = [st.enter_context(nc.psum_tensor(f"p1ps{i}", [128, TC], F32)) for i in range(2)]
    pball = [banks.b[i][0] for i in range(6)] + pst

    dl = P.sem(); dlg = P.sem(); dwl = P.sem(); dwlg = P.sem(); dco = P.sem()
    do = P.sem(); dog = P.sem()
    pe = P.sem(); dv = P.sem(); ac = P.sem(); po = P.sem()

    with ExitStack() as st_w:
        _, bf1 = emit_weff(nc, st_w, P, io["w_fcT"], io["weff_fc"], 1024, 4096, "p1wf",
                           (pe, dv, ac), (dwl, dco), pball)
    with ExitStack() as st_w:
        _, bf2 = emit_weff(nc, st_w, P, io["w_projT"], io["weff_pj"], 4096, 1024, "p1wp",
                           (pe, dv, ac), (dwl, dco), pball)
    for i in range(6):
        banks.release(i, *bf2)
    pst_free = bf2

    vt16 = sb(nc, st, "p1vt16", (128, KC * 16), BF16)
    fb16 = sb(nc, st, "p1fb16", (32, 16), BF16)
    ut16 = sb(nc, st, "p1ut16", (32, 4096), BF16)
    gt = sb(nc, st, "p1gt", (16, 1))
    P.wait("gpsimd", dco)
    P.wait("gpsimd", dv, dv.v)
    P.do("gpsimd", lambda e: e.memset(fb16[:, :], 0.0), inc=po)
    P.do("gpsimd", lambda e: e.memset(ut16[:, :], 0.0), inc=po)
    P.wait("gpsimd", po, po.v)
    P.dma("gpsimd", vt16[:], io["vT"], dlg)
    P.dma("gpsimd", fb16[0:16, :], io["fbT"], dlg)
    P.dma("gpsimd", ut16[0:16, :], io["uT"], dlg)
    P.dma("sync", gt[:], io["gate16"], dl)
    t_smalls = dl.v
    t_smalls_g = dlg.v

    lns = ln_scratch(nc, st, "p1ln", TC)
    ln_scratch_init(P, lns, dv)

    xc = [sb(nc, st, "p1xc0", (128, KC * TC))]
    xn0 = sb(nc, st, "p1xn0", (128, KC * TC), BF16)
    xn1 = sb(nc, st, "p1xn1", (128, KC * TC), BF16)
    sbt = sb(nc, st, "p1sb", (128, KC * TC))
    s0 = sb(nc, st, "p1s0", (128, KC * TC))
    g0 = sb(nc, st, "p1g0", (128, KC * TC))
    x1t = sb(nc, st, "p1x1t", (128, KC * TC))
    rtmp = sb(nc, st, "p1rtmp", (128, TC))
    rr = sb(nc, st, "p1rr", (128, MF * TC), BF16)
    t1b = sb(nc, st, "p1t1b", (32, TC), BF16)
    t2g = sb(nc, st, "p1t2g", (32, TC), BF16)
    P.do("vector", lambda e: e.memset(t1b[:, :], 0.0), inc=dv)
    P.do("vector", lambda e: e.memset(t2g[:, :], 0.0), inc=dv)
    wfgrp = [sb(nc, st, f"p1wfg{i}", (128, KC * 4 * 128), BF16) for i in range(2)]
    wpb = [sb(nc, st, f"p1wpb{i}", (128, MF * 128), BF16) for i in range(2)]
    xzev = [sb(nc, st, f"p1xzev{i}", (128, TC)) for i in range(2)]

    xc_free = [None, None]
    sf_free = None; sb_free = None; s0_free = None; g0_free = None
    x1_free = None; x1_dma_free = None; rr_free = None; xn0_free = None; xn1_free = None
    wfg_free = [None, None]
    wpb_free = [None, None]
    xzev_free = [None, None]
    lnxb_free = None

    for c in range(NCH):
        t0 = c * TC
        cb = 0
        if xc_free[cb] is not None:
            P.wait("sync", *xc_free[cb])
        P.dma("sync", xc[cb][:].rearrange("p (k t) -> p k t", t=TC),
              io["xh"][:, t0:t0 + TC].rearrange("(k p) t -> p k t", p=128), dl)
        if sf_free is not None:
            P.wait("sync", *sf_free)
            P.wait("sync", *sb_free)
        P.dma("sync", s0[:].rearrange("p (k t) -> p k t", t=TC),
              io["sfh"][:, t0:t0 + TC].rearrange("(k p) t -> p k t", p=128), dl)
        P.dma("sync", sbt[:].rearrange("p (k t) -> p k t", t=TC),
              io["sbh"][:, t0:t0 + TC].rearrange("(k p) t -> p k t", p=128), dl)
        x_ld = dl.v
        pre = [("vector", dl, x_ld), ("scalar", dl, x_ld)]
        if lnxb_free is not None:
            pre.append(("vector", *lnxb_free))
            pre.append(("scalar", *lnxb_free))
        if pst_free is not None:
            pre.append(("tensor", *pst_free))
        if xn0_free is not None:
            pre.append(("vector", *xn0_free))
        t_xn0, t_st0 = emit_ln(P, xc[cb][:], xn0[:], lns, pst, (pe, dv, ac), TC, pre=pre)
        P.wait("gpsimd", dl, x_ld)
        if s0_free is not None:
            P.wait("gpsimd", *s0_free)
        t_s0 = P.dof("gpsimd", lambda e: e.tensor_tensor(out=s0[:], in0=s0[:], in1=sbt[:],
                                                         op=ALU.add), inc=po)
        sf_free = (dv, 0); sb_free = (po, t_s0)
        bi_l, bt_l = banks.acquire(P, "tensor")
        P.wait("tensor", dv, t_xn0)
        if c == 0:
            P.wait("tensor", dl, t_smalls)
            P.wait("tensor", dlg, t_smalls_g)
        for kc in range(KC):
            P.do("tensor", lambda e, kc=kc, bt_l=bt_l, cb=cb: e.matmul(
                bt_l[0:16, :], vt16[:, kc * 16:(kc + 1) * 16], xn0[:, kc * TC:(kc + 1) * TC],
                start=(kc == 0), stop=(kc == KC - 1)), inc=pe if kc == KC - 1 else None)
        t_t1 = pe.v
        P.wait("vector", pe, t_t1)
        t_t1e = P.dof("vector", lambda e, bt_l=bt_l: e.tensor_copy(out=t1b[0:16, :], in_=bt_l[0:16, :]), inc=dv)
        banks.release(bi_l, dv, t_t1e)
        bi_l2, bt_l2 = banks.acquire(P, "tensor")
        P.wait("tensor", dv, t_t1e)
        P.do("tensor", lambda e, bt_l2=bt_l2: e.matmul(
            bt_l2[0:16, :], fb16[:, :], t1b[0:32, :], start=True, stop=True), inc=pe)
        t_t2 = pe.v
        P.wait("vector", pe, t_t2)
        t_t2e = P.dof("vector", lambda e, bt_l2=bt_l2: e.tensor_scalar(
            out=t2g[0:16, :], in0=bt_l2[0:16, :], scalar1=gt[0:16, 0:1], scalar2=None,
            op0=ALU.mult), inc=dv)
        banks.release(bi_l2, dv, t_t2e)
        if io.get("dbg"):
            P.wait("gpsimd", dv, t_t2e)
            P.dma("gpsimd", io["dbg"]["t1o"][:, t0:t0 + TC], t1b[0:16, :], do)
            P.dma("gpsimd", io["dbg"]["t2o"][:, t0:t0 + TC], t2g[0:16, :], do)
        if rr_free is not None:
            P.wait("scalar", *rr_free)
        for grp in GRP:
            gb = (c * len(GRP) + GRP.index(grp)) % 2
            gw = len(grp) * 128
            if wfg_free[gb] is not None:
                P.wait("sync", *wfg_free[gb])
            P.dma("sync",
                  wfgrp[gb][:, 0:KC * gw].rearrange("p (kc j) -> p kc j", j=gw),
                  io["weff_fc"][:, grp[0] * 128: grp[0] * 128 + gw]
                  .rearrange("(kc p) j -> p kc j", p=128), dwl)
            t_wf = dwl.v
            P.wait("tensor", dwl, t_wf)
            P.wait("tensor", dv, t_t2e)
            grp_last_mm = None
            nolr = bool(os.environ.get("P1NOLR"))
            for mi, M in enumerate(grp):
                bi, bt = banks.acquire(P, "tensor")
                for kc in range(KC):
                    P.do("tensor", lambda e, kc=kc, bt=bt, gb=gb, mi=mi, gw=gw, nolr=nolr: e.matmul(
                        bt[:, :], wfgrp[gb][:, kc * gw + mi * 128: kc * gw + (mi + 1) * 128],
                        xn0[:, kc * TC:(kc + 1) * TC], start=(kc == 0),
                        stop=(nolr and kc == KC - 1)),
                        inc=pe if (nolr and kc == KC - 1) else None)
                if not nolr:
                    P.do("tensor", lambda e, bt=bt, M=M: e.matmul(
                        bt[:, :], ut16[0:32, M * 128:(M + 1) * 128], t2g[0:32, :],
                        start=False, stop=True), inc=pe)
                t_mm = pe.v
                grp_last_mm = t_mm
                P.wait("scalar", pe, t_mm)
                P.do("scalar", lambda e, bt=bt: e.activation(out=rtmp[:], in_=bt[:, :],
                                                             func=AF.Relu), inc=ac)
                fn_rr = lambda e, M=M: e.activation(
                    out=rr[:, M * TC:(M + 1) * TC], in_=rtmp[:], func=AF.Square)
                t_rr = (P.dof("scalar", fn_rr, inc=ac) if M == MF - 1
                        else P.do("scalar", fn_rr, inc=ac))
                banks.release(bi, ac, t_rr)
            wfg_free[gb] = (pe, grp_last_mm)
        t_rr_all = ac.v
        xn0_free = (pe, pe.v)
        if g0_free is not None:
            P.wait("scalar", *g0_free)
        for M in range(MP):
            pb = (c * MP + M) % 2
            if wpb_free[pb] is not None:
                P.wait("sync", *wpb_free[pb])
            P.dma("sync",
                  wpb[pb][:].rearrange("p (kc j) -> p kc j", j=128),
                  io["weff_pj"][:, M * 128:(M + 1) * 128]
                  .rearrange("(kc p) j -> p kc j", p=128), dwl)
            t_wp = dwl.v
            bi, bt = banks.acquire(P, "tensor")
            P.wait("tensor", dwl, t_wp)
            if M == 0:
                P.wait("tensor", ac, t_rr_all)
            for kc in range(MF):
                P.do("tensor", lambda e, kc=kc, bt=bt, pb=pb: e.matmul(
                    bt[:, :], wpb[pb][:, kc * 128:(kc + 1) * 128],
                    rr[:, kc * TC:(kc + 1) * TC], start=(kc == 0), stop=(kc == MF - 1)),
                    inc=pe if kc == MF - 1 else None)
            t_mm = pe.v
            wpb_free[pb] = (pe, t_mm)
            P.wait("scalar", pe, t_mm)
            fn_g0 = lambda e, bt=bt, M=M: e.activation(
                out=g0[:, M * TC:(M + 1) * TC], in_=bt[:, :], func=AF.Silu)
            t_g0 = (P.dof("scalar", fn_g0, inc=ac) if M == MP - 1
                    else P.do("scalar", fn_g0, inc=ac))
            banks.release(bi, ac, t_g0)
        t_g0_all = ac.v
        rr_free = (pe, pe.v)
        P.wait("vector", ac, t_g0_all)
        P.wait("vector", po, t_s0)
        if x1_free is not None:
            P.wait("vector", *x1_free)
            P.wait("vector", *x1_dma_free)
        for k in range(KC):
            P.do("vector", lambda e, k=k: e.tensor_tensor(
                out=s0[:, k * TC:(k + 1) * TC], in0=s0[:, k * TC:(k + 1) * TC],
                in1=g0[:, k * TC:(k + 1) * TC], op=ALU.mult), inc=dv)
            fn_x1 = lambda e, k=k, cb=cb: e.tensor_tensor(
                out=x1t[:, k * TC:(k + 1) * TC], in0=s0[:, k * TC:(k + 1) * TC],
                in1=xc[cb][:, k * TC:(k + 1) * TC], op=ALU.add)
            (P.dof("vector", fn_x1, inc=dv) if k == KC - 1 else P.do("vector", fn_x1, inc=dv))
        t_x1 = dv.v
        if io.get("dbg"):
            P.wait("gpsimd", dv, t_x1)
            P.wait("gpsimd", ac, t_g0_all)
            P.dma("gpsimd", io["dbg"]["g0o"][:, t0:t0 + TC].rearrange("(k p) t -> p k t", p=128),
                  g0[:].rearrange("p (k t) -> p k t", t=TC), do)
            P.dma("gpsimd", io["dbg"]["s0o"][:, t0:t0 + TC].rearrange("(k p) t -> p k t", p=128),
                  s0[:].rearrange("p (k t) -> p k t", t=TC), do)
            P.wait("gpsimd", dv, t_xn0)
            P.dma("gpsimd", io["dbg"]["xn0o"][:, t0:t0 + TC].rearrange("(k p) t -> p k t", p=128),
                  xn0[:].rearrange("p (k t) -> p k t", t=TC), do)
            P.wait("gpsimd", ac, t_rr_all)
            P.dma("gpsimd", io["dbg"]["rro"][:, t0:t0 + TC].rearrange("(k p) t -> p k t", p=128),
                  rr[:].rearrange("p (k t) -> p k t", t=TC), do)
        s0_free = (dv, t_x1)
        sf_free = (dv, t_x1)
        g0_free = (dv, t_x1)
        xc_free[cb] = (dv, t_x1)
        P.wait("sync", dv, t_x1)
        P.dma("sync", io["x1o"][:, t0:t0 + TC].rearrange("(k p) t -> p k t", p=128),
              x1t[:].rearrange("p (k t) -> p k t", t=TC), do)
        t_x1dma = do.v
        pre = [("vector", dv, t_x1), ("scalar", dv, t_x1),
               ("vector", ac, t_g0_all),
               ("vector", pe, t_st0), ("scalar", pe, t_st0),
               ("tensor", dv, t_x1)]
        if xn1_free is not None:
            pre.append(("vector", *xn1_free))
        t_xn1, t_st1 = emit_ln(P, x1t[:], xn1[:], lns, pst, (pe, dv, ac), TC, pre=pre)
        lnxb_free = (pe, t_st1)
        pst_free = (dv, t_xn1)
        x1_free = (dv, t_xn1)
        x1_dma_free = (do, t_x1dma)
        for grp in GRP:
            gb = (c * len(GRP) + GRP.index(grp)) % 2
            gw = len(grp) * 128
            if wfg_free[gb] is not None:
                P.wait("gpsimd", *wfg_free[gb])
            P.dma("gpsimd",
                  wfgrp[gb][:, 0:KC * gw].rearrange("p (kc j) -> p kc j", j=gw),
                  io["W_in"][:, grp[0] * 128: grp[0] * 128 + gw]
                  .rearrange("(kc p) j -> p kc j", p=128), dwlg)
            t_wf = dwlg.v
            P.wait("tensor", dwlg, t_wf)
            P.wait("tensor", dv, t_xn1)
            grp_last_mm = None
            for mi, M in enumerate(grp):
                bi, bt = banks.acquire(P, "tensor")
                for kc in range(KC):
                    P.do("tensor", lambda e, kc=kc, bt=bt, gb=gb, mi=mi, gw=gw: e.matmul(
                        bt[:, :], wfgrp[gb][:, kc * gw + mi * 128: kc * gw + (mi + 1) * 128],
                        xn1[:, kc * TC:(kc + 1) * TC], start=(kc == 0), stop=(kc == KC - 1)),
                        inc=pe if kc == KC - 1 else None)
                t_mm = pe.v
                grp_last_mm = t_mm
                xb = (c * MF + M) % 2
                P.wait("vector", pe, t_mm)
                if xzev_free[xb] is not None:
                    P.wait("vector", *xzev_free[xb])
                t_ev = P.dof("vector", lambda e, bt=bt, xb=xb: e.tensor_copy(
                    out=xzev[xb][:], in_=bt[:, :]), inc=dv)
                banks.release(bi, dv, t_ev)
                P.wait("gpsimd", dv, t_ev)
                t_o = P.dma("gpsimd", io["xzo"][M * 128:(M + 1) * 128, t0:t0 + TC],
                            xzev[xb][:], dog)
                xzev_free[xb] = (dog, t_o)
            wfg_free[gb] = (pe, grp_last_mm)
        xn1_free = (pe, pe.v)

    P.wait("sync", do)
    P.wait("gpsimd", dog)
    return P


def stage_p1(inputs, p0_parts, x=None):
    if x is None:
        x = np.asarray(inputs["L0_attn_x"])
    TH = x.shape[1] // 2
    w_fc = np.asarray(inputs["L0_ffn_w_fc"]); w_proj = np.asarray(inputs["L0_ffn_w_proj"])
    maps = []
    for core in range(8):
        b, p = core // 2, core & 1
        sl = slice(p * TH, (p + 1) * TH)
        maps.append({
            "xh": np.ascontiguousarray(x[b].T[:, sl]),
            "sfh": np.ascontiguousarray(p0_parts[2 * b][:, sl]),
            "sbh": np.ascontiguousarray(p0_parts[2 * b + 1][:, sl]),
            "w_fcT": np.ascontiguousarray(w_fc.T),
            "w_projT": np.ascontiguousarray(w_proj.T),
            "vT": np.ascontiguousarray(
                np.asarray(inputs["L0_ffn_v"]).T.reshape(8, 128, 16)
                .transpose(1, 0, 2).reshape(128, 128)),
            "fbT": np.ascontiguousarray(np.asarray(inputs["L0_ffn_fast_b"]).T),
            "uT": np.ascontiguousarray(np.asarray(inputs["L0_ffn_u"]).T),
            "gate16": np.full((16, 1), np.float32(inputs["L0_ffn_gate"]), np.float32),
            "W_in": np.ascontiguousarray(np.asarray(inputs["L1_attn_W_in"])),
        })
    return maps


def build_p2(TH=2048, TC=512):
    nc = bass.Bass("TRN2", target_bir_lowering=False)
    xzin = nc.dram_tensor("xzin", [2048, TH + 3], F32, kind="ExternalInput").ap()
    zin = nc.dram_tensor("zin", [2048, TH], F32, kind="ExternalInput").ap()
    cw = nc.dram_tensor("cw", [128, 64], F32, kind="ExternalInput").ap()
    cb = nc.dram_tensor("cb", [128, 16], F32, kind="ExternalInput").ap()
    wx = nc.dram_tensor("wx", [128, 16 * 96], F32, kind="ExternalInput").ap()
    xmo = nc.dram_tensor("xmo", [2048, TH], F32, kind="ExternalOutput").ap()
    zso = nc.dram_tensor("zso", [2048, TH], F32, kind="ExternalOutput").ap()
    xdblo = nc.dram_tensor("xdblo", [96, TH], F32, kind="ExternalOutput").ap()
    with ExitStack() as st:
        P = Prog(nc, st)
        emit_p2(nc, st, P, dict(xzin=xzin, zin=zin, cw=cw, cb=cb, wx=wx,
                                xmo=xmo, zso=zso, xdblo=xdblo), TH, TC)
        with nc.Block() as block:
            P.emit(block)
    return nc


def emit_p2(nc, st, P, io, TH, TC):
    MC = 16
    NCH = TH // TC
    TE = TC + 3
    xmr = [sb(nc, st, f"p2xmr{i}", (128, MC * TE)) for i in range(2)]
    xm = sb(nc, st, "p2xm", (128, MC * TC))

    zt = [sb(nc, st, f"p2zt{i}", (128, MC * TC)) for i in range(2)]
    acc = [sb(nc, st, f"p2acc{i}", (128, TC)) for i in range(2)]
    cwt = sb(nc, st, "p2cw", (128, 64))
    cbt = sb(nc, st, "p2cb", (128, 16))
    wx16 = sb(nc, st, "p2wx", (128, MC * 96))
    xdb = [sb(nc, st, f"p2xdb{i}", (96, TC)) for i in range(2)]
    pbank = [st.enter_context(nc.psum_tensor(f"p2pb{i}", [128, TC], F32)) for i in range(2)]

    dl = P.sem(); dlg = P.sem(); do = P.sem()
    pe = P.sem(); dv = P.sem(); ac = P.sem(); po = P.sem()
    P.dma("sync", cwt[:], io["cw"], dl)
    P.dma("sync", cbt[:], io["cb"], dl)
    P.dma("gpsimd", wx16[:], io["wx"], dlg)
    t_smalls = dl.v
    t_smalls_g = dlg.v

    xmr_free = [None, None]
    zt_free = [None, None]
    xm_free = None
    xmb_free = None
    xdb_free = [None, None]
    pb_free = [None, None]
    for c in range(NCH):
        cb_ = c % 2
        t0 = c * TC
        if xmr_free[cb_] is not None:
            P.wait("sync", *xmr_free[cb_])
        P.dma("sync", xmr[cb_][:].rearrange("p (k t) -> p k t", t=TE),
              io["xzin"][:, t0:t0 + TE].rearrange("(k p) t -> p k t", p=128), dl)
        if zt_free[cb_] is not None:
            P.wait("sync", *zt_free[cb_])
        P.dma("sync", zt[cb_][:].rearrange("p (k t) -> p k t", t=TC),
              io["zin"][:, t0:t0 + TC].rearrange("(k p) t -> p k t", p=128), dl)
        x_ld = dl.v
        P.wait("vector", dl, x_ld)
        if c == 0:
            P.wait("vector", dl, t_smalls)
            P.wait("scalar", dl, t_smalls)
            P.wait("tensor", dlg, t_smalls_g)
        if xm_free is not None:
            for eng_ctr in xm_free:
                P.wait("scalar", *eng_ctr)
        acc_free = [None, None]
        for mc in range(MC):
            base = mc * TE
            ab = mc % 2
            if acc_free[ab] is not None:
                P.wait("vector", *acc_free[ab])
            P.do("vector", lambda e, cb_=cb_, base=base, mc=mc, ab=ab: e.tensor_scalar(
                out=acc[ab][:], in0=xmr[cb_][:, base:base + TC], scalar1=cwt[:, 0 * 16 + mc:0 * 16 + mc + 1],
                scalar2=None, op0=ALU.mult), inc=dv)
            for j in range(1, 4):
                fn_cv = lambda e, cb_=cb_, base=base, mc=mc, j=j, ab=ab: e.scalar_tensor_tensor(
                    out=acc[ab][:], in0=xmr[cb_][:, base + j:base + j + TC],
                    scalar=cwt[:, j * 16 + mc:j * 16 + mc + 1], in1=acc[ab][:],
                    op0=ALU.mult, op1=ALU.add)
                (P.dof("vector", fn_cv, inc=dv) if j == 3 else P.do("vector", fn_cv, inc=dv))
            t_acc = dv.v
            P.wait("scalar", dv, t_acc)
            fn_si = lambda e, mc=mc, ab=ab: e.activation(
                out=xm[:, mc * TC:(mc + 1) * TC], in_=acc[ab][:], func=AF.Silu,
                bias=cbt[:, mc:mc + 1])
            t_si = (P.dof("scalar", fn_si, inc=ac) if mc == MC - 1
                    else P.do("scalar", fn_si, inc=ac))
            acc_free[ab] = (ac, t_si)
        t_xm = ac.v
        xmr_free[cb_] = (ac, t_xm)
        P.wait("sync", ac, t_xm)
        P.dma("sync", io["xmo"][:, t0:t0 + TC].rearrange("(k p) t -> p k t", p=128),
              xm[:].rearrange("p (k t) -> p k t", t=TC), do)
        t_xm_out = do.v
        P.wait("tensor", ac, t_xm)
        pb_i = c % 2
        if pb_free[pb_i] is not None:
            P.wait("tensor", *pb_free[pb_i])
        for kc in range(MC):
            P.do("tensor", lambda e, kc=kc, pb_i=pb_i: e.matmul(
                pbank[pb_i][0:96, :], wx16[:, kc * 96:(kc + 1) * 96],
                xm[:, kc * TC:(kc + 1) * TC], start=(kc == 0), stop=(kc == MC - 1)),
                inc=pe if kc == MC - 1 else None)
        t_mm = pe.v
        xmb_free = (pe, t_mm)
        t_xmb = t_mm
        if xdb_free[pb_i] is not None:
            P.wait("vector", *xdb_free[pb_i])
        P.wait("vector", pe, t_mm)
        t_ev = P.dof("vector", lambda e, pb_i=pb_i: e.tensor_copy(
            out=xdb[pb_i][:], in_=pbank[pb_i][0:96, :]), inc=dv)
        pb_free[pb_i] = (dv, t_ev)
        xm_free = [(do, t_xm_out), (pe, t_xmb)]
        P.wait("sync", dv, t_ev)
        t_od = P.dma("sync", io["xdblo"][:, t0:t0 + TC], xdb[pb_i][:], do)
        xdb_free[pb_i] = (do, t_od)
        P.wait("scalar", dl, x_ld)
        for mc in range(MC):
            fn_zs = lambda e, cb_=cb_, mc=mc: e.activation(
                out=zt[cb_][:, mc * TC:(mc + 1) * TC], in_=zt[cb_][:, mc * TC:(mc + 1) * TC],
                func=AF.Silu)
            (P.dof("scalar", fn_zs, inc=ac) if mc == MC - 1 else P.do("scalar", fn_zs, inc=ac))
        t_zs2 = ac.v
        P.wait("sync", ac, t_zs2)
        P.dma("sync", io["zso"][:, t0:t0 + TC].rearrange("(k p) t -> p k t", p=128),
              zt[cb_][:].rearrange("p (k t) -> p k t", t=TC), do)
        zt_free[cb_] = (do, do.v)

    P.wait("sync", do)
    return P


def stage_p2(inputs, xz_full):
    conv_w = np.asarray(inputs["L1_attn_conv_w"]).reshape(4, 2048)
    cw = np.concatenate([conv_w[j].reshape(16, 128).T for j in range(4)], axis=1)
    cbv = np.asarray(inputs["L1_attn_conv_b"]).reshape(16, 128).T
    wx = np.asarray(inputs["L1_attn_W_x"])
    wxh = wx.reshape(16, 128, 96).transpose(1, 0, 2).reshape(128, 16 * 96)
    T = xz_full[0].shape[1]
    TH = T // 2
    maps = []
    for core in range(8):
        b, p = core // 2, core & 1
        lo, hi = p * TH, (p + 1) * TH
        xm_rows = xz_full[b][0:2048]
        pad = np.zeros((2048, 1), np.float32)
        left = xm_rows[:, lo - 1:lo] if lo > 0 else pad
        right = xm_rows[:, hi:hi + 2]
        if right.shape[1] < 2:
            right = np.concatenate([right, np.zeros((2048, 2 - right.shape[1]), np.float32)], 1)
        xzin = np.concatenate([left, xm_rows[:, lo:hi], right], axis=1)
        maps.append({
            "xzin": np.ascontiguousarray(xzin),
            "zin": np.ascontiguousarray(xz_full[b][2048:4096, lo:hi]),
            "cw": np.ascontiguousarray(cw.astype(np.float32)),
            "cb": np.ascontiguousarray(cbv.astype(np.float32)),
            "wx": np.ascontiguousarray(wxh.astype(np.float32)),
        })
    return maps


def build_p3(T=4096, TS=512):
    nc = bass.Bass("TRN2", target_bir_lowering=False)
    xmh = nc.dram_tensor("xmh", [1024, T], F32, kind="ExternalInput").ap()
    zsh = nc.dram_tensor("zsh", [1024, T], F32, kind="ExternalInput").ap()
    xdbl = nc.dram_tensor("xdbl", [96, T], F32, kind="ExternalInput").ap()
    wdt = nc.dram_tensor("wdt", [64, 1024], F32, kind="ExternalInput").ap()
    bdt = nc.dram_tensor("bdt", [128, 8], F32, kind="ExternalInput").ap()
    A_h = nc.dram_tensor("A_h", [128, 128], F32, kind="ExternalInput").ap()
    D_h = nc.dram_tensor("D_h", [128, 8], F32, kind="ExternalInput").ap()
    wout = nc.dram_tensor("wout", [1024, 1024], F32, kind="ExternalInput").ap()
    s1part = nc.dram_tensor("s1part", [1024, T], F32, kind="ExternalOutput").ap()
    with ExitStack() as st:
        P = Prog(nc, st)
        emit_p3(nc, st, P, dict(xmh=xmh, zsh=zsh, xdbl=xdbl, wdt=wdt, bdt=bdt,
                                A_h=A_h, D_h=D_h, wout=wout, s1part=s1part), T, TS)
        with nc.Block() as block:
            P.emit(block)
    return nc


def emit_p3(nc, st, P, io, T, TS):
    MC = 8
    NSG = 2
    NCH = T // TS
    KC = 8

    xm = [sb(nc, st, "p3xm0", (128, MC * TS))]
    zs = sb(nc, st, "p3zs", (128, MC * TS))
    dt = sb(nc, st, "p3dt", (128, MC * TS))
    dtl = [sb(nc, st, f"p3dtl{i}", (64, TS)) for i in range(2)]
    bc32 = [sb(nc, st, f"p3bc32{i}", (32, TS)) for i in range(2)]
    qiota = sb(nc, st, "p3qiota", (32, 128))
    qiotai = sb(nc, st, "p3qiotai", (32, 128), I32)
    sel = sb(nc, st, "p3sel", (32, 128))
    Bb = sb(nc, st, "p3Bb", (128, 8 * TS), BF16)
    Cb = sb(nc, st, "p3Cb", (128, 8 * TS), BF16)
    a_g = [sb(nc, st, f"p3a{i}", (128, 8 * TS)) for i in range(2)]
    b_g = [sb(nc, st, f"p3b{i}", (128, 8 * TS)) for i in range(2)]
    st_g = sb(nc, st, "p3st", (128, 8 * TS))
    dtxt = [sb(nc, st, f"p3dtx{i}", (128, TS)) for i in range(2)]
    y_t = sb(nc, st, "p3y", (128, MC * TS))
    yg16 = sb(nc, st, "p3yg", (128, MC * TS), BF16)
    mstate = sb(nc, st, "p3ms", (128, 128))
    wout16 = sb(nc, st, "p3wo", (128, KC * 1024), BF16)
    wdt16 = sb(nc, st, "p3wdt", (64, MC * 128))
    A_t = sb(nc, st, "p3A", (128, 128))
    D_t = sb(nc, st, "p3D", (128, 8))
    bdt_t = sb(nc, st, "p3bdt", (128, 8))
    nbdt = sb(nc, st, "p3nbdt", (128, 8))
    aat = [sb(nc, st, f"p3aat{i}", (128, TS)) for i in range(2)]

    banks = Banks(nc, st, 8, TS, "p3pb")

    dl = P.sem(); dlg = P.sem(); do = P.sem()
    pe = P.sem(); dv = P.sem(); ac = P.sem(); po = P.sem()

    P.dma("gpsimd", wout16[:].rearrange("p (kc j) -> p kc j", j=1024),
          io["wout"].rearrange("(kc p) j -> p kc j", p=128), dlg)
    P.dma("sync", wdt16[:], io["wdt"], dl)
    P.dma("sync", A_t[:], io["A_h"], dl)
    P.dma("sync", D_t[:], io["D_h"], dl)
    P.dma("sync", bdt_t[:], io["bdt"], dl)
    t_res = dl.v
    t_res_g = dlg.v
    P.wait("vector", dl, t_res)
    P.wait("scalar", dl, t_res)
    P.do("scalar", lambda e: e.activation(out=A_t[:], in_=A_t[:], func=AF.Exp), inc=ac)
    t_Aexp = ac.v
    P.wait("vector", ac, t_Aexp)
    P.do("vector", lambda e: e.tensor_scalar(out=A_t[:], in0=A_t[:], scalar1=-1.0,
                                             scalar2=None, op0=ALU.mult), inc=dv)
    P.dof("vector", lambda e: e.tensor_scalar(out=nbdt[:], in0=bdt_t[:], scalar1=-1.0,
                                              scalar2=None, op0=ALU.mult), inc=dv)
    P.do("vector", lambda e: e.memset(mstate[:], 0.0), inc=dv)
    t_init = dv.v
    P.do("gpsimd", lambda e: e.iota(qiotai[:], pattern=[[0, 128]], base=0,
                                    channel_multiplier=1), inc=po)
    t_iota = po.v
    P.wait("vector", po, t_iota)
    P.do("vector", lambda e: e.tensor_copy(out=qiota[:], in_=qiotai[:]), inc=dv)
    t_qiota = dv.v
    P.wait("vector", dv, t_qiota)

    xm_free = [None, None]
    zs_free = None
    dt_free = None
    dtl_free = [None, None]
    bc_free = [None, None]
    ab_free = [None, None]
    bgf = [None, None]
    dtxt_free = [None, None]
    bg_free = None
    sel_free = None
    Bb_free = None
    Cb_free = None
    yg_free = None
    spt_free = None
    for c in range(NCH):
        cb_ = c % 2
        t0 = c * TS
        if xm_free[0] is not None:
            P.wait("sync", *xm_free[0])
        P.dma("sync", xm[0][:].rearrange("p (k t) -> p k t", t=TS),
              io["xmh"][:, t0:t0 + TS].rearrange("(k p) t -> p k t", p=128), dl)
        t_xm_ld = dl.v
        if zs_free is not None:
            P.wait("sync", *zs_free)
        P.dma("sync", zs[:].rearrange("p (k t) -> p k t", t=TS),
              io["zsh"][:, t0:t0 + TS].rearrange("(k p) t -> p k t", p=128), dl)
        t_zs_ld = dl.v
        if dtl_free[cb_] is not None:
            P.wait("sync", *dtl_free[cb_])
        P.dma("sync", dtl[cb_][:], io["xdbl"][0:64, t0:t0 + TS], dl)
        t_dtl = dl.v
        if bc_free[cb_] is not None:
            P.wait("sync", *bc_free[cb_])
        P.dma("sync", bc32[cb_][:], io["xdbl"][64:96, t0:t0 + TS], dl)
        t_bc = dl.v

        if dt_free is not None:
            for w in dt_free:
                P.wait("scalar", *w)
        for mc in range(MC):
            bi, bt = banks.acquire(P, "tensor")
            if mc == 0:
                P.wait("tensor", dl, t_dtl)
                if c == 0:
                    P.wait("tensor", dl, t_res)
                    P.wait("tensor", dlg, t_res_g)
            P.do("tensor", lambda e, mc=mc, bt=bt, cb_=cb_: e.matmul(
                bt[:, :], wdt16[:, mc * 128:(mc + 1) * 128], dtl[cb_][:, :],
                start=True, stop=True), inc=pe)
            t_mm = pe.v
            sb_ = mc % 2
            P.wait("scalar", pe, t_mm)
            P.do("scalar", lambda e, bt=bt, mc=mc: e.activation(
                out=dt[:, mc * TS:(mc + 1) * TS], in_=bt[:, :], func=AF.Relu,
                bias=bdt_t[:, mc:mc + 1]), inc=ac)
            t_a1 = P.do("scalar", lambda e, bt=bt, mc=mc, sb_=sb_: e.activation(
                out=aat[sb_][:], in_=bt[:, :], func=AF.Abs, scale=-1.0,
                bias=nbdt[:, mc:mc + 1]), inc=ac)
            banks.release(bi, ac, t_a1)
            P.do("scalar", lambda e, sb_=sb_: e.activation(
                out=aat[sb_][:], in_=aat[sb_][:], func=AF.Exp, scale=-1.0), inc=ac)
            t_l = P.dof("scalar", lambda e, sb_=sb_: e.activation(
                out=aat[sb_][:], in_=aat[sb_][:], func=AF.Ln, bias=1.0), inc=ac)
            P.wait("vector", ac, t_l)
            fn_dt = lambda e, mc=mc, sb_=sb_: e.tensor_tensor(
                out=dt[:, mc * TS:(mc + 1) * TS], in0=dt[:, mc * TS:(mc + 1) * TS],
                in1=aat[sb_][:], op=ALU.add)
            (P.dof("vector", fn_dt, inc=dv) if mc == MC - 1 else P.do("vector", fn_dt, inc=dv))
        t_dt = dv.v

        for sg in range(NSG):
            P.wait("tensor", dl, t_bc)
            if c == 0:
                P.wait("tensor", dv, t_qiota)
            for si in range(8):
                s = sg * 8 + si
                for j, (dstt, row) in enumerate(((Bb, s), (Cb, 16 + s))):
                    if sel_free is not None:
                        P.wait("vector", *sel_free)
                    P.dof("vector", lambda e, row=row: e.tensor_scalar(
                        out=sel[:], in0=qiota[:], scalar1=float(row), scalar2=None,
                        op0=ALU.is_equal), inc=dv)
                    t_sel = dv.v
                    bi, bt = banks.acquire(P, "tensor")
                    P.wait("tensor", dv, t_sel)
                    P.do("tensor", lambda e, bt=bt, cb_=cb_: e.matmul(
                        bt[:, :], sel[:, :], bc32[cb_][:, :], start=True, stop=True), inc=pe)
                    t_mmb = pe.v
                    sel_free = (pe, t_mmb)
                    P.wait("vector", pe, t_mmb)
                    fn_ev = lambda e, dstt=dstt, si=si, bt=bt: e.tensor_copy(
                        out=dstt[:, si * TS:(si + 1) * TS], in_=bt[:, :])
                    t_ev = (P.dof("vector", fn_ev, inc=dv) if (si == 7 and j == 1)
                            else P.do("vector", fn_ev, inc=dv))
                    banks.release(bi, dv, t_ev)
            t_bcb = dv.v

            for mc in range(MC):
                db_ = mc % 2
                P.wait("gpsimd", dv, t_dt)
                if c == 0 and mc == 0 and sg == 0:
                    pass
                P.wait("gpsimd", dl, t_xm_ld)
                if dtxt_free[db_] is not None:
                    P.wait("gpsimd", *dtxt_free[db_])
                P.do("gpsimd", lambda e, mc=mc, db_=db_: e.tensor_tensor(
                    out=dtxt[db_][:], in0=dt[:, mc * TS:(mc + 1) * TS],
                    in1=xm[0][:, mc * TS:(mc + 1) * TS], op=ALU.mult), inc=po)
                t_dtx = po.v
                u = (mc * NSG + sg) % 2
                if ab_free[u] is not None:
                    P.wait("scalar", *ab_free[u])
                P.wait("scalar", dv, t_dt)
                if c == 0 and sg == 0 and mc == 0:
                    P.wait("scalar", dv, t_init)
                for si in range(8):
                    s = sg * 8 + si
                    fn_ag = lambda e, mc=mc, si=si, s=s, u=u: e.activation(
                        out=a_g[u][:, si * TS:(si + 1) * TS], in_=dt[:, mc * TS:(mc + 1) * TS],
                        func=AF.Exp, scale=A_t[:, mc * 16 + s:mc * 16 + s + 1])
                    (P.dof("scalar", fn_ag, inc=ac) if si == 7 else P.do("scalar", fn_ag, inc=ac))
                t_ab = ac.v
                if bgf[u] is not None:
                    P.wait("gpsimd", *bgf[u])
                P.wait("gpsimd", dv, t_bcb)
                for si in range(8):
                    fn_bg = lambda e, si=si, db_=db_, u=u: e.tensor_tensor(
                        out=b_g[u][:, si * TS:(si + 1) * TS], in0=dtxt[db_][:],
                        in1=Bb[:, si * TS:(si + 1) * TS], op=ALU.mult)
                    (P.dof("gpsimd", fn_bg, inc=po) if si == 7 else P.do("gpsimd", fn_bg, inc=po))
                t_bb = po.v
                dtxt_free[db_] = (po, t_bb)
                P.wait("vector", ac, t_ab)
                P.wait("vector", po, t_bb)
                for si in range(8):
                    s = sg * 8 + si
                    P.do("vector", lambda e, si=si, mc=mc, s=s, u=u: e.tensor_tensor_scan(
                        out=st_g[:, si * TS:(si + 1) * TS], data0=a_g[u][:, si * TS:(si + 1) * TS],
                        data1=b_g[u][:, si * TS:(si + 1) * TS],
                        initial=mstate[:, mc * 16 + s:mc * 16 + s + 1],
                        op0=ALU.mult, op1=ALU.add), inc=dv)
                t_scans = dv.v
                bgf[u] = (dv, t_scans)
                for si in range(8):
                    P.do("vector", lambda e, mc=mc, sg=sg, si=si: e.tensor_copy(
                        out=mstate[:, mc * 16 + sg * 8 + si: mc * 16 + sg * 8 + si + 1],
                        in_=st_g[:, si * TS + TS - 1: si * TS + TS]), inc=dv)
                P.do("vector", lambda e, u=u: e.tensor_tensor(
                    out=a_g[u][:, :], in0=st_g[:, :], in1=Cb[:, :], op=ALU.mult), inc=dv)
                P.do("vector", lambda e, u=u: e.tensor_tensor(
                    out=a_g[u][:, 0:4 * TS], in0=a_g[u][:, 0:4 * TS], in1=a_g[u][:, 4 * TS:8 * TS],
                    op=ALU.add), inc=dv)
                P.do("vector", lambda e, u=u: e.tensor_tensor(
                    out=a_g[u][:, 0:2 * TS], in0=a_g[u][:, 0:2 * TS], in1=a_g[u][:, 2 * TS:4 * TS],
                    op=ALU.add), inc=dv)
                if sg == 0:
                    P.do("vector", lambda e, mc=mc, u=u: e.tensor_tensor(
                        out=y_t[:, mc * TS:(mc + 1) * TS], in0=a_g[u][:, 0:TS],
                        in1=a_g[u][:, TS:2 * TS], op=ALU.add), inc=dv)
                else:
                    P.do("vector", lambda e, u=u: e.tensor_tensor(
                        out=a_g[u][:, 0:TS], in0=a_g[u][:, 0:TS], in1=a_g[u][:, TS:2 * TS],
                        op=ALU.add), inc=dv)
                    P.do("vector", lambda e, mc=mc, u=u: e.tensor_tensor(
                        out=y_t[:, mc * TS:(mc + 1) * TS], in0=y_t[:, mc * TS:(mc + 1) * TS],
                        in1=a_g[u][:, 0:TS], op=ALU.add), inc=dv)
                t_tree = dv.v
                ab_free[u] = (dv, t_tree)
        P.wait("vector", dl, t_zs_ld)
        if yg_free is not None:
            P.wait("vector", *yg_free)
        for mc in range(MC):
            P.do("vector", lambda e, mc=mc: e.scalar_tensor_tensor(
                out=y_t[:, mc * TS:(mc + 1) * TS], in0=xm[0][:, mc * TS:(mc + 1) * TS],
                scalar=D_t[:, mc:mc + 1], in1=y_t[:, mc * TS:(mc + 1) * TS],
                op0=ALU.mult, op1=ALU.add), inc=dv)
            fn_yg = lambda e, mc=mc: e.tensor_tensor(
                out=yg16[:, mc * TS:(mc + 1) * TS], in0=y_t[:, mc * TS:(mc + 1) * TS],
                in1=zs[:, mc * TS:(mc + 1) * TS], op=ALU.mult)
            (P.dof("vector", fn_yg, inc=dv) if mc == MC - 1 else P.do("vector", fn_yg, inc=dv))
        t_yg = dv.v
        xm_free[0] = (dv, t_yg)
        zs_free = (dv, t_yg)
        dt_free = [(ac, ac.v), (po, po.v)]
        P.wait("tensor", dv, t_yg)
        for M in range(KC):
            bi, bt = banks.acquire(P, "tensor")
            for kc in range(MC):
                P.do("tensor", lambda e, M=M, kc=kc, bt=bt: e.matmul(
                    bt[:, :], wout16[:, kc * 1024 + M * 128: kc * 1024 + (M + 1) * 128],
                    yg16[:, kc * TS:(kc + 1) * TS], start=(kc == 0), stop=(kc == MC - 1)),
                    inc=pe if kc == MC - 1 else None)
            t_mm = pe.v
            P.wait("vector", pe, t_mm)
            if spt_free is not None and M == 0:
                P.wait("vector", *spt_free)
            fn_ye = lambda e, M=M, bt=bt: e.tensor_copy(
                out=y_t[:, M * TS:(M + 1) * TS], in_=bt[:, :])
            t_ev = (P.dof("vector", fn_ye, inc=dv) if M == KC - 1 else P.do("vector", fn_ye, inc=dv))
            banks.release(bi, dv, t_ev)
        t_sp = dv.v
        yg_free = (pe, pe.v)
        P.wait("sync", dv, t_sp)
        P.dma("sync", io["s1part"][:, t0:t0 + TS].rearrange("(k p) t -> p k t", p=128),
              y_t[:].rearrange("p (k t) -> p k t", t=TS), do)
        spt_free = (do, do.v)
    P.wait("sync", do)
    return P


def stage_p3(inputs, xm_full, zs_full, xdbl_full):
    wdt = np.asarray(inputs["L1_attn_W_dt"])
    bdt = np.asarray(inputs["L1_attn_b_dt"])
    A_log = np.asarray(inputs["L1_attn_A_log"])
    Dv = np.asarray(inputs["L1_attn_D"])
    woutw = np.asarray(inputs["L1_attn_W_out"])
    maps = []
    for core in range(8):
        b, p = core // 2, core & 1
        rs = slice(p * 1024, (p + 1) * 1024)
        A_sl = A_log[rs]
        A_h = A_sl.reshape(8, 128, 16).transpose(1, 0, 2).reshape(128, 128)
        maps.append({
            "xmh": np.ascontiguousarray(xm_full[b][rs]),
            "zsh": np.ascontiguousarray(zs_full[b][rs]),
            "xdbl": np.ascontiguousarray(xdbl_full[b]),
            "wdt": np.ascontiguousarray(wdt[:, rs]),
            "bdt": np.ascontiguousarray(bdt[rs].reshape(8, 128).T),
            "A_h": np.ascontiguousarray(A_h.astype(np.float32)),
            "D_h": np.ascontiguousarray(Dv[rs].reshape(8, 128).T),
            "wout": np.ascontiguousarray(woutw[rs]),
        })
    return maps


def build_p4(TH=2048, TC=512):
    nc = bass.Bass("TRN2", target_bir_lowering=False)
    x1h = nc.dram_tensor("x1h", [1024, TH], F32, kind="ExternalInput").ap()
    s1a = nc.dram_tensor("s1a", [1024, TH], F32, kind="ExternalInput").ap()
    s1b = nc.dram_tensor("s1b", [1024, TH], F32, kind="ExternalInput").ap()
    rw = nc.dram_tensor("rw", [128, 64], F32, kind="ExternalInput").ap()
    gw = nc.dram_tensor("gw", [8 * 1024, 1024], F32, kind="ExternalInput").ap()
    uw = nc.dram_tensor("uw", [8 * 1024, 1024], F32, kind="ExternalInput").ap()
    dw = nc.dram_tensor("dw", [8 * 1024, 1024], F32, kind="ExternalInput").ap()
    shg = nc.dram_tensor("shg", [1024, 2048], F32, kind="ExternalInput").ap()
    shu = nc.dram_tensor("shu", [1024, 2048], F32, kind="ExternalInput").ap()
    shd = nc.dram_tensor("shd", [2048, 1024], F32, kind="ExternalInput").ap()
    x2o = nc.dram_tensor("x2o", [1024, TH], F32, kind="ExternalOutput").ap()
    _os2 = os
    cmbTo = (nc.dram_tensor("cmbTo", [32, TH], F32, kind="ExternalOutput").ap()
             if _os2.environ.get("P4DEBUG") else None)
    with ExitStack() as st:
        P = Prog(nc, st)
        emit_p4(nc, st, P, dict(x1h=x1h, s1a=s1a, s1b=s1b, rw=rw, gw=gw, uw=uw, dw=dw,
                                shg=shg, shu=shu, shd=shd, x2o=x2o, cmbTo=cmbTo), TH, TC)
        with nc.Block() as block:
            P.emit(block)
    return nc


def emit_p4(nc, st, P, io, TH, TC):
    KC = 8
    NCH = TH // TC
    NE = 8
    HT = TH // 2
    NTH = HT // TC
    xn1b = sb(nc, st, "p4xn1", (128, NCH * KC * TC), BF16)
    rw16 = sb(nc, st, "p4rw", (128, 64), BF16)
    qi32 = sb(nc, st, "p4qi", (32, 128))
    qi32i = sb(nc, st, "p4qii", (32, 128), I32)
    sel = sb(nc, st, "p4sel", (32, 128))
    cmbT = sb(nc, st, "p4cmbT", (32, TH))
    xc = sb(nc, st, "p4xc", (128, KC * TC))

    banks = Banks(nc, st, 6, TC, "p4pb")
    pst = [st.enter_context(nc.psum_tensor(f"p4ps{i}", [128, TC], F32)) for i in range(2)]

    dl = P.sem(); dlg = P.sem(); dwl = P.sem(); do = P.sem()
    pe = P.sem(); dv = P.sem(); ac = P.sem(); po = P.sem()

    t_rw = P.dma("gpsimd", rw16[:], io["rw"], dlg)
    P.do("gpsimd", lambda e: e.iota(qi32i[:], pattern=[[0, 128]], base=0,
                                    channel_multiplier=1), inc=po)
    t_io = po.v
    P.wait("vector", po, t_io)
    P.do("vector", lambda e: e.tensor_copy(out=qi32[:], in_=qi32i[:]), inc=dv)
    P.wait("vector", dv, dv.v)
    P.do("vector", lambda e: e.memset(cmbT[:], 0.0), inc=dv)
    t_pre = dv.v

    with ExitStack() as st_l:
        lns = ln_scratch(nc, st_l, "p4ln", TC)
        sc8 = sb(nc, st_l, "p4sc8", (128, 8))
        tm1 = sb(nc, st_l, "p4tm1", (128, 8))
        tm2 = sb(nc, st_l, "p4tm2", (128, 8))
        msk = sb(nc, st_l, "p4msk", (128, 8))
        cmb16 = sb(nc, st_l, "p4cmb16", (128, 8), BF16)
        ieye = sb(nc, st_l, "p4ieye", (128, 128), I32)
        irow = sb(nc, st_l, "p4irow", (128, 1))
        irowf = sb(nc, st_l, "p4irowf", (128, 1), I32)
        eye16 = sb(nc, st_l, "p4eye", (128, 128), BF16)
        ln_scratch_init(P, lns, dv)
        P.do("gpsimd", lambda e: e.iota(ieye[:], pattern=[[1, 128]], base=0,
                                        channel_multiplier=0), inc=po)
        P.do("gpsimd", lambda e: e.iota(irowf[:], pattern=[[0, 1]], base=0,
                                        channel_multiplier=1), inc=po)
        t_io2 = po.v
        P.wait("vector", po, t_io2)
        P.do("vector", lambda e: e.tensor_copy(out=irow[:], in_=irowf[:]), inc=dv)
        P.wait("vector", dv, dv.v)
        P.do("vector", lambda e: e.tensor_scalar(out=eye16[:], in0=ieye[:],
                                                 scalar1=irow[:, 0:1], scalar2=None,
                                                 op0=ALU.is_equal), inc=dv)
        t_eye = dv.v

        xc_free = None
        pst_free = None
        lnxb_free = None
        for c in range(NCH):
            t0 = c * TC
            if xc_free is not None:
                P.wait("sync", *xc_free)
            P.dma("sync", xc[:].rearrange("p (k t) -> p k t", t=TC),
                  io["x1h"][:, t0:t0 + TC].rearrange("(k p) t -> p k t", p=128), dl)
            x_ld = dl.v
            pre = [("vector", dl, x_ld), ("scalar", dl, x_ld)]
            if lnxb_free is not None:
                pre.append(("vector", *lnxb_free)); pre.append(("scalar", *lnxb_free))
            if pst_free is not None:
                pre.append(("tensor", *pst_free))
            out_b = xn1b[:, c * KC * TC:(c + 1) * KC * TC]
            t_xn, t_st = emit_ln(P, xc[:], out_b, lns, pst, (pe, dv, ac), TC, pre=pre)
            xc_free = (dv, t_xn)
            lnxb_free = (pe, t_st)
            pst_free = (dv, t_xn)
            for o in range(0, TC, 128):
                bi, bt = banks.acquire(P, "tensor")
                P.wait("tensor", dv, t_xn)
                if c == 0 and o == 0:
                    P.wait("tensor", dlg, t_rw)
                    P.wait("tensor", dv, t_eye)
                for kc in range(KC):
                    P.do("tensor", lambda e, c=c, kc=kc, o=o, bt=bt: e.matmul(
                        bt[0:128, 0:8],
                        xn1b[:, c * KC * TC + kc * TC + o: c * KC * TC + kc * TC + o + 128],
                        rw16[:, kc * 8:(kc + 1) * 8], start=(kc == 0), stop=(kc == KC - 1)),
                        inc=pe if kc == KC - 1 else None)
                t_mm = pe.v
                P.wait("scalar", pe, t_mm)
                t_exp = P.dof("scalar", lambda e, bt=bt: e.activation(
                    out=sc8[:], in_=bt[0:128, 0:8], func=AF.Exp), inc=ac)
                banks.release(bi, ac, t_exp)
                P.wait("vector", ac, t_exp)
                def vop(fn):
                    P.wait("vector", dv, dv.v)
                    return P.do("vector", fn, inc=dv)
                vop(lambda e: e.reduce_sum(out=tm1[:, 0:1], in_=sc8[:], axis=AX.X))
                vop(lambda e: e.reciprocal(out=tm1[:, 0:1], in_=tm1[:, 0:1]))
                vop(lambda e: e.tensor_copy(out=tm1[:, 1:2], in_=tm1[:, 0:1]))
                vop(lambda e: e.tensor_copy(out=tm1[:, 2:4], in_=tm1[:, 0:2]))
                vop(lambda e: e.tensor_copy(out=tm1[:, 4:8], in_=tm1[:, 0:4]))
                vop(lambda e: e.tensor_tensor(out=sc8[:], in0=sc8[:], in1=tm1[:, 0:8], op=ALU.mult))
                vop(lambda e: e.reduce_max(out=tm2[:, 0:1], in_=sc8[:], axis=AX.X))
                vop(lambda e: e.tensor_copy(out=tm2[:, 1:2], in_=tm2[:, 0:1]))
                vop(lambda e: e.tensor_copy(out=tm2[:, 2:4], in_=tm2[:, 0:2]))
                vop(lambda e: e.tensor_copy(out=tm2[:, 4:8], in_=tm2[:, 0:4]))
                vop(lambda e: e.tensor_tensor(out=msk[:], in0=sc8[:], in1=tm2[:, 0:8], op=ALU.is_ge))
                vop(lambda e: e.scalar_tensor_tensor(out=msk[:], in0=msk[:], scalar=-1e30,
                                                     in1=sc8[:], op0=ALU.mult, op1=ALU.add))
                vop(lambda e: e.reduce_max(out=tm2[:, 0:1], in_=msk[:], axis=AX.X))
                vop(lambda e: e.tensor_copy(out=tm2[:, 1:2], in_=tm2[:, 0:1]))
                vop(lambda e: e.tensor_copy(out=tm2[:, 2:4], in_=tm2[:, 0:2]))
                vop(lambda e: e.tensor_copy(out=tm2[:, 4:8], in_=tm2[:, 0:4]))
                vop(lambda e: e.tensor_tensor(out=msk[:], in0=sc8[:], in1=tm2[:, 0:8], op=ALU.is_ge))
                vop(lambda e: e.tensor_tensor(out=sc8[:], in0=sc8[:], in1=msk[:], op=ALU.mult))
                P.wait("vector", dv, dv.v)
                P.dof("vector", lambda e: e.tensor_copy(out=cmb16[:], in_=sc8[:]), inc=dv)
                t_cmb = dv.v
                bi2, bt2 = banks.acquire(P, "tensor")
                P.wait("tensor", dv, t_cmb)
                P.do("tensor", lambda e, bt2=bt2: e.transpose(
                    bt2.bitcast(BF16)[0:8, 0:128], cmb16[:, :], eye16[:, :]), inc=pe)
                t_tp = pe.v
                P.wait("vector", pe, t_tp)
                t_ct = P.do("vector", lambda e, t0=t0, o=o, bt2=bt2: e.tensor_copy(
                    out=cmbT[0:8, t0 + o:t0 + o + 128], in_=bt2.bitcast(BF16)[0:8, 0:128]), inc=dv)
                banks.release(bi2, dv, t_ct)
        t_cmbT = dv.v
        t_scores_pe = pe.v
        t_scores_ac = ac.v

    if io.get("cmbTo") is not None:
        P.wait("sync", dv, t_cmbT)
        P.dma("sync", io["cmbTo"], cmbT[:], do)
    P.wait("gpsimd", dv, t_cmbT)
    P.wait("gpsimd", pe, t_scores_pe)
    P.wait("gpsimd", ac, t_scores_ac)
    P.wait("sync", dv, t_cmbT)
    cmb_be = sb(nc, st, "p4cmbbe", (128, HT))
    he = sb(nc, st, "p4he", (128, KC * HT), BF16)
    hsh = sb(nc, st, "p4hsh", (128, 16 * TC), BF16)
    sg_t = [sb(nc, st, f"p4sg{i}", (128, TC)) for i in range(2)]
    wg = [sb(nc, st, f"p4wg{i}", (128, KC * 1024), BF16) for i in range(3)]
    racc = sb(nc, st, "p4racc", (128, KC * HT))
    x1t2 = sb(nc, st, "p4x1t2", (128, KC * TC))

    he_free = None
    cmbbe_free = None
    sel_free = None
    wg_free = [None] * 3
    sgt_free = [None, None]
    hsh_free = None
    out_free = None
    _os = os
    NE_RUN = 0 if _os.environ.get("P4NOEXP") else NE
    NOSH = bool(_os.environ.get("P4NOSH"))
    for half in range(2):
        hbase = half * HT
        racc_init = False
        if NE_RUN == 0:
            P.do("vector", lambda e: e.memset(racc[:], 0.0), inc=dv)
            racc_init = True
        for e_i in range(NE_RUN):
            wb = e_i % 2
            if wg_free[wb] is not None:
                P.wait("gpsimd", *wg_free[wb])
            P.dma("gpsimd", wg[wb][:].rearrange("p (kc j) -> p kc j", j=1024),
                  io["gw"][e_i * 1024:(e_i + 1) * 1024, :].rearrange("(kc p) j -> p kc j", p=128),
                  dwl)
            if wg_free[2] is not None:
                P.wait("gpsimd", *wg_free[2])
            P.dma("gpsimd", wg[2][:].rearrange("p (kc j) -> p kc j", j=1024),
                  io["uw"][e_i * 1024:(e_i + 1) * 1024, :].rearrange("(kc p) j -> p kc j", p=128),
                  dwl)
            t_gw = dwl.v
            if sel_free is not None:
                P.wait("vector", *sel_free)
            P.do("vector", lambda e, e_i=e_i: e.tensor_scalar(
                out=sel[:], in0=qi32[:], scalar1=float(e_i), scalar2=None,
                op0=ALU.is_equal), inc=dv)
            t_sel = dv.v
            if cmbbe_free is not None:
                P.wait("vector", *cmbbe_free)
            for nt in range(NTH):
                bi, bt = banks.acquire(P, "tensor")
                P.wait("tensor", dv, t_sel)
                P.do("tensor", lambda e, nt=nt, bt=bt, hbase=hbase: e.matmul(
                    bt[:, :], sel[:, :], cmbT[0:32, hbase + nt * TC: hbase + (nt + 1) * TC],
                    start=True, stop=True), inc=pe)
                t_mb = pe.v
                sel_free = (pe, t_mb)
                P.wait("vector", pe, t_mb)
                t_ev = P.do("vector", lambda e, nt=nt, bt=bt: e.tensor_copy(
                    out=cmb_be[:, nt * TC:(nt + 1) * TC], in_=bt[:, :]), inc=dv)
                banks.release(bi, dv, t_ev)
            t_cbe = dv.v
            P.wait("tensor", dwl, t_gw)
            if he_free is not None:
                P.wait("vector", *he_free)
                he_free = None
            for fc in range(KC):
                for nt in range(NTH):
                    xoff = (hbase // TC + nt) * KC * TC
                    big, btg = banks.acquire(P, "tensor")
                    for kc in range(KC):
                        P.do("tensor", lambda e, fc=fc, kc=kc, btg=btg, wb=wb, xoff=xoff: e.matmul(
                            btg[:, :], wg[wb][:, kc * 1024 + fc * 128: kc * 1024 + (fc + 1) * 128],
                            xn1b[:, xoff + kc * TC: xoff + (kc + 1) * TC],
                            start=(kc == 0), stop=(kc == KC - 1)),
                            inc=pe if kc == KC - 1 else None)
                    t_g = pe.v
                    sb_i = (fc * NTH + nt) % 2
                    P.wait("scalar", pe, t_g)
                    if sgt_free[sb_i] is not None:
                        P.wait("scalar", *sgt_free[sb_i])
                    t_sg = P.do("scalar", lambda e, btg=btg, sb_i=sb_i: e.activation(
                        out=sg_t[sb_i][:], in_=btg[:, :], func=AF.Silu), inc=ac)
                    banks.release(big, ac, t_sg)
                    biu, btu = banks.acquire(P, "tensor")
                    for kc in range(KC):
                        P.do("tensor", lambda e, fc=fc, kc=kc, btu=btu, xoff=xoff: e.matmul(
                            btu[:, :], wg[2][:, kc * 1024 + fc * 128: kc * 1024 + (fc + 1) * 128],
                            xn1b[:, xoff + kc * TC: xoff + (kc + 1) * TC],
                            start=(kc == 0), stop=(kc == KC - 1)),
                            inc=pe if kc == KC - 1 else None)
                    t_u = pe.v
                    P.wait("vector", pe, t_u)
                    P.wait("vector", ac, t_sg)
                    P.wait("vector", dv, t_cbe)
                    P.do("vector", lambda e, btu=btu, sb_i=sb_i: e.tensor_tensor(
                        out=sg_t[sb_i][:], in0=sg_t[sb_i][:], in1=btu[:, :], op=ALU.mult), inc=dv)
                    if _os.environ.get("P4NOCMB"):
                        fn_he = lambda e, fc=fc, nt=nt, sb_i=sb_i: e.tensor_copy(
                            out=he[:, fc * HT + nt * TC:fc * HT + (nt + 1) * TC],
                            in_=sg_t[sb_i][:])
                    else:
                        fn_he = lambda e, fc=fc, nt=nt, sb_i=sb_i: e.tensor_tensor(
                            out=he[:, fc * HT + nt * TC:fc * HT + (nt + 1) * TC],
                            in0=sg_t[sb_i][:], in1=cmb_be[:, nt * TC:(nt + 1) * TC],
                            op=ALU.mult)
                    t_h = (P.dof("vector", fn_he, inc=dv) if (fc == KC - 1 and nt == NTH - 1)
                           else P.do("vector", fn_he, inc=dv))
                    banks.release(biu, dv, t_h)
                    sgt_free[sb_i] = (dv, t_h)
            t_he = dv.v
            cmbbe_free = (dv, t_he)
            wg_free[wb] = None
            P.wait("gpsimd", pe, pe.v)
            P.dma("gpsimd", wg[wb][:].rearrange("p (kc j) -> p kc j", j=1024),
                  io["dw"][e_i * 1024:(e_i + 1) * 1024, :].rearrange("(kc p) j -> p kc j", p=128),
                  dwl)
            t_dw = dwl.v
            P.wait("tensor", dwl, t_dw)
            P.wait("tensor", dv, t_he)
            for hc in range(KC):
                for nt in range(NTH):
                    bi, bt = banks.acquire(P, "tensor")
                    for fc in range(KC):
                        P.do("tensor", lambda e, hc=hc, nt=nt, fc=fc, bt=bt, wb=wb: e.matmul(
                            bt[:, :], wg[wb][:, fc * 1024 + hc * 128: fc * 1024 + (hc + 1) * 128],
                            he[:, fc * HT + nt * TC:fc * HT + (nt + 1) * TC],
                            start=(fc == 0), stop=(fc == KC - 1)),
                            inc=pe if fc == KC - 1 else None)
                    t_d = pe.v
                    P.wait("vector", pe, t_d)
                    if not racc_init and half >= 0:
                        pass
                    if not racc_init:
                        t_r = P.do("vector", lambda e, hc=hc, nt=nt, bt=bt: e.tensor_copy(
                            out=racc[:, hc * HT + nt * TC: hc * HT + (nt + 1) * TC],
                            in_=bt[:, :]), inc=dv)
                    else:
                        t_r = P.do("vector", lambda e, hc=hc, nt=nt, bt=bt: e.tensor_tensor(
                            out=racc[:, hc * HT + nt * TC: hc * HT + (nt + 1) * TC],
                            in0=racc[:, hc * HT + nt * TC: hc * HT + (nt + 1) * TC],
                            in1=bt[:, :], op=ALU.add), inc=dv)
                    banks.release(bi, dv, t_r)
            racc_init = True
            he_free = (pe, pe.v)
            wg_free[wb] = (pe, pe.v)
            wg_free[2] = (pe, pe.v)
        for nt in range(NTH if not NOSH else 0):
            xoff = (hbase // TC + nt) * KC * TC
            for mb in range(16):
                wb = mb % 2
                if wg_free[wb] is not None:
                    P.wait("gpsimd", *wg_free[wb])
                P.dma("gpsimd", wg[wb][:, 0:KC * 128].rearrange("p (kc j) -> p kc j", j=128),
                      io["shg"][:, mb * 128:(mb + 1) * 128]
                      .rearrange("(kc p) j -> p kc j", p=128), dwl)
                P.dma("gpsimd", wg[wb][:, KC * 128:2 * KC * 128].rearrange("p (kc j) -> p kc j", j=128),
                      io["shu"][:, mb * 128:(mb + 1) * 128]
                      .rearrange("(kc p) j -> p kc j", p=128), dwl)
                t_w = dwl.v
                big, btg = banks.acquire(P, "tensor")
                P.wait("tensor", dwl, t_w)
                for kc in range(KC):
                    P.do("tensor", lambda e, kc=kc, btg=btg, wb=wb, xoff=xoff: e.matmul(
                        btg[:, :], wg[wb][:, kc * 128: (kc + 1) * 128],
                        xn1b[:, xoff + kc * TC: xoff + (kc + 1) * TC],
                        start=(kc == 0), stop=(kc == KC - 1)),
                        inc=pe if kc == KC - 1 else None)
                t_g = pe.v
                sb_i = mb % 2
                P.wait("scalar", pe, t_g)
                if sgt_free[sb_i] is not None:
                    P.wait("scalar", *sgt_free[sb_i])
                t_sg = P.do("scalar", lambda e, btg=btg, sb_i=sb_i: e.activation(
                    out=sg_t[sb_i][:], in_=btg[:, :], func=AF.Silu), inc=ac)
                banks.release(big, ac, t_sg)
                biu, btu = banks.acquire(P, "tensor")
                for kc in range(KC):
                    P.do("tensor", lambda e, kc=kc, btu=btu, wb=wb, xoff=xoff: e.matmul(
                        btu[:, :], wg[wb][:, KC * 128 + kc * 128: KC * 128 + (kc + 1) * 128],
                        xn1b[:, xoff + kc * TC: xoff + (kc + 1) * TC],
                        start=(kc == 0), stop=(kc == KC - 1)),
                        inc=pe if kc == KC - 1 else None)
                t_u = pe.v
                P.wait("vector", pe, t_u)
                P.wait("vector", ac, t_sg)
                if hsh_free is not None and mb == 0:
                    P.wait("vector", *hsh_free)
                fn_hs = lambda e, mb=mb, btu=btu, sb_i=sb_i: e.tensor_tensor(
                    out=hsh[:, mb * TC:(mb + 1) * TC], in0=sg_t[sb_i][:], in1=btu[:, :],
                    op=ALU.mult)
                t_h = (P.dof("vector", fn_hs, inc=dv) if mb == 15 else P.do("vector", fn_hs, inc=dv))
                banks.release(biu, dv, t_h)
                sgt_free[sb_i] = (dv, t_h)
                wg_free[wb] = (pe, pe.v)
            t_hsh = dv.v
            for wb in (0, 1):
                if wg_free[wb] is not None:
                    P.wait("gpsimd", *wg_free[wb])
                P.dma("gpsimd", wg[wb][:].rearrange("p (kc j) -> p kc j", j=1024),
                      io["shd"][wb * 1024:(wb + 1) * 1024, :]
                      .rearrange("(kc p) j -> p kc j", p=128), dwl)
            t_w = dwl.v
            P.wait("tensor", dwl, t_w)
            P.wait("tensor", dv, t_hsh)
            for hc in range(KC):
                bi, bt = banks.acquire(P, "tensor")
                for fc in range(16):
                    wb2, fcl = fc // 8, fc % 8
                    P.do("tensor", lambda e, hc=hc, fc=fc, bt=bt, wb2=wb2, fcl=fcl: e.matmul(
                        bt[:, :], wg[wb2][:, fcl * 1024 + hc * 128: fcl * 1024 + (hc + 1) * 128],
                        hsh[:, fc * TC:(fc + 1) * TC],
                        start=(fc == 0), stop=(fc == 15)),
                        inc=pe if fc == 15 else None)
                t_d = pe.v
                P.wait("vector", pe, t_d)
                t_r = P.do("vector", lambda e, hc=hc, nt=nt, bt=bt: e.tensor_tensor(
                    out=racc[:, hc * HT + nt * TC: hc * HT + (nt + 1) * TC],
                    in0=racc[:, hc * HT + nt * TC: hc * HT + (nt + 1) * TC],
                    in1=bt[:, :], op=ALU.add), inc=dv)
                banks.release(bi, dv, t_r)
            hsh_free = (pe, pe.v)
            wg_free[0] = (pe, pe.v)
            wg_free[1] = (pe, pe.v)
        P.dof("vector", lambda e: e.tensor_copy(out=sel[0:1, 0:1], in_=sel[0:1, 1:2]), inc=dv)
        t_racc = dv.v
        P.wait("scalar", dv, t_racc)
        for hc in range(KC):
            fn_g1 = lambda e, hc=hc: e.activation(
                out=racc[:, hc * HT:(hc + 1) * HT], in_=racc[:, hc * HT:(hc + 1) * HT],
                func=AF.Silu)
            (P.dof("scalar", fn_g1, inc=ac) if hc == KC - 1 else P.do("scalar", fn_g1, inc=ac))
        t_g1 = ac.v
        for nt in range(NTH):
            t0 = hbase + nt * TC
            if out_free is not None:
                P.wait("sync", *out_free)
            P.dma("sync", xc[:].rearrange("p (k t) -> p k t", t=TC),
                  io["s1a"][:, t0:t0 + TC].rearrange("(k p) t -> p k t", p=128), dl)
            P.dma("sync", x1t2[:].rearrange("p (k t) -> p k t", t=TC),
                  io["s1b"][:, t0:t0 + TC].rearrange("(k p) t -> p k t", p=128), dl)
            t_s1 = dl.v
            P.wait("gpsimd", dl, t_s1)
            t_add = P.dof("gpsimd", lambda e: e.tensor_tensor(out=xc[:], in0=xc[:],
                                                              in1=x1t2[:], op=ALU.add), inc=po)
            P.wait("sync", po, t_add)
            P.dma("sync", x1t2[:].rearrange("p (k t) -> p k t", t=TC),
                  io["x1h"][:, t0:t0 + TC].rearrange("(k p) t -> p k t", p=128), dl)
            t_x1 = dl.v
            P.wait("vector", dl, t_x1)
            P.wait("vector", po, t_add)
            P.wait("vector", ac, t_g1)
            for k in range(KC):
                P.do("vector", lambda e, k=k, nt=nt: e.tensor_tensor(
                    out=xc[:, k * TC:(k + 1) * TC], in0=xc[:, k * TC:(k + 1) * TC],
                    in1=racc[:, k * HT + nt * TC: k * HT + (nt + 1) * TC], op=ALU.mult), inc=dv)
                fn_x2 = lambda e, k=k: e.tensor_tensor(
                    out=xc[:, k * TC:(k + 1) * TC], in0=xc[:, k * TC:(k + 1) * TC],
                    in1=x1t2[:, k * TC:(k + 1) * TC], op=ALU.add)
                (P.dof("vector", fn_x2, inc=dv) if k == KC - 1 else P.do("vector", fn_x2, inc=dv))
            t_x2 = dv.v
            P.wait("sync", dv, t_x2)
            P.dma("sync", io["x2o"][:, t0:t0 + TC].rearrange("(k p) t -> p k t", p=128),
                  xc[:].rearrange("p (k t) -> p k t", t=TC), do)
            out_free = (do, do.v)
    P.wait("sync", do)
    P.wait("gpsimd", do)
    return P


def stage_p4(inputs, x1_full, s1_parts):
    T = x1_full[0].shape[1]
    TH = T // 2
    rw = np.asarray(inputs["L1_ffn_router_w"])
    rwp = rw.reshape(8, 128, 8).transpose(1, 0, 2).reshape(128, 64)
    gw = np.asarray(inputs["L1_ffn_gate_w"]).reshape(8 * 1024, 1024)
    uw = np.asarray(inputs["L1_ffn_up_w"]).reshape(8 * 1024, 1024)
    dw = np.asarray(inputs["L1_ffn_down_w"]).reshape(8 * 1024, 1024)
    maps = []
    for core in range(8):
        b, p = core // 2, core & 1
        sl = slice(p * TH, (p + 1) * TH)
        maps.append({
            "x1h": np.ascontiguousarray(x1_full[b][:, sl]),
            "s1a": np.ascontiguousarray(s1_parts[2 * b][:, sl]),
            "s1b": np.ascontiguousarray(s1_parts[2 * b + 1][:, sl]),
            "rw": np.ascontiguousarray(rwp.astype(np.float32)),
            "gw": np.ascontiguousarray(gw), "uw": np.ascontiguousarray(uw),
            "dw": np.ascontiguousarray(dw),
            "shg": np.ascontiguousarray(np.asarray(inputs["L1_ffn_shared_gate_w"])),
            "shu": np.ascontiguousarray(np.asarray(inputs["L1_ffn_shared_up_w"])),
            "shd": np.ascontiguousarray(np.asarray(inputs["L1_ffn_shared_down_w"])),
        })
    return maps


def kernel(**inputs):
    T = 4096
    nc0 = build_p0(T=T)
    res0 = run_bass_kernel_spmd(nc0, stage_p0(inputs), core_ids=list(range(8)))
    parts = unstage_p0(res0.results)

    nc1 = build_p1(TH=T // 2, TC=512)
    res1 = run_bass_kernel_spmd(nc1, stage_p1(inputs, parts), core_ids=list(range(8)))
    x1_full, xz_full = [], []
    for b in range(4):
        x1_full.append(np.concatenate([np.asarray(res1.results[2 * b]["x1o"]),
                                       np.asarray(res1.results[2 * b + 1]["x1o"])], axis=1))
        xz_full.append(np.concatenate([np.asarray(res1.results[2 * b]["xzo"]),
                                       np.asarray(res1.results[2 * b + 1]["xzo"])], axis=1))

    nc2 = build_p2(TH=T // 2, TC=512)
    res2 = run_bass_kernel_spmd(nc2, stage_p2(inputs, xz_full), core_ids=list(range(8)))
    xm_full, zs_full, xdbl_full = [], [], []
    for b in range(4):
        xm_full.append(np.concatenate([np.asarray(res2.results[2 * b]["xmo"]),
                                       np.asarray(res2.results[2 * b + 1]["xmo"])], axis=1))
        zs_full.append(np.concatenate([np.asarray(res2.results[2 * b]["zso"]),
                                       np.asarray(res2.results[2 * b + 1]["zso"])], axis=1))
        xdbl_full.append(np.concatenate([np.asarray(res2.results[2 * b]["xdblo"]),
                                         np.asarray(res2.results[2 * b + 1]["xdblo"])], axis=1))

    nc3 = build_p3(T=T, TS=512)
    res3 = run_bass_kernel_spmd(nc3, stage_p3(inputs, xm_full, zs_full, xdbl_full),
                                core_ids=list(range(8)))
    s1_parts = [np.asarray(res3.results[c]["s1part"]) for c in range(8)]

    nc4 = build_p4(TH=T // 2, TC=512)
    res4 = run_bass_kernel_spmd(nc4, stage_p4(inputs, x1_full, s1_parts),
                                core_ids=list(range(8)))

    out = np.zeros((4, T, 1024), np.float32)
    for core in range(8):
        b, p = core // 2, core & 1
        out[b, p * (T // 2):(p + 1) * (T // 2)] = np.asarray(res4.results[core]["x2o"]).T
    return out
